# version 24 (final)
# baseline (speedup 1.0000x reference)
import math
import numpy as np
import ml_dtypes

import concourse.bass as bass
import concourse.mybir as mybir
from concourse.bass_utils import run_bass_kernel_spmd

F32 = mybir.dt.float32
BF16 = mybir.dt.bfloat16
AF = mybir.ActivationFunctionType
ALU = mybir.AluOpType
AX = mybir.AxisListType

T = 2048
D = 4096
DQ = 1024
HPAD = 2816
NG = 88
SHIFT = 12.0
ISQ = 1.0 / math.sqrt(128.0)
EPS = 1e-5

BF_COLS = 69632
C_XNF = 0
C_K = 32768
C_WQ = 49152
C_QEV = 57344
C_M1 = 59392
C_M2 = 61440
C_SPILL = 63488
C_COS = 65536
C_V = 0
C_PRB = 16384
C_QS = 18432
C_CTXO = 19456
C_MSK = 20480
C_XF = 0
C_YF = 8192
C_W1 = 30720
C_W3 = 32768
C_GATE = 34816
C_YT = 35840
C_W2 = 36864
C_DEV = 42496
C_CTX = 0
C_WO = 32768
C_DEV2 = 40960
C_DBK = 43008
C_DTM = 45056
C_LNJ = 0
C_LNO = 4096
C_XFB = 49152

F32_COLS = 9216


def _rope_tables():
    idx = np.arange(0, 128, 2, dtype=np.float64) / 128.0
    freq = 1.0 / (2.0 * math.pi) * (500000.0 ** (-idx))
    factor, lo, hi = 32.0, 1.0, 4.0
    L0 = 8192.0
    freq_low, freq_high = lo / L0, hi / L0
    freq_scaled = np.where(freq < freq_low, freq / factor, freq)
    smooth = np.clip((L0 * freq - lo) / (hi - lo), 0.0, 1.0)
    freq_smooth = (1.0 - smooth) * (freq / factor) + smooth * freq
    is_mid = (freq >= freq_low) & (freq <= freq_high)
    freq = np.where(is_mid, freq_smooth, freq_scaled)
    phase = 2.0 * math.pi * np.arange(T, dtype=np.float64)[:, None] * freq[None, :]
    c = np.cos(phase).T.astype(np.float32)
    s = np.sin(phase).T.astype(np.float32)
    return (np.ascontiguousarray(np.concatenate([c, c], axis=0)),
            np.ascontiguousarray(np.concatenate([-s, s], axis=0)))


def _img(w, din, dout):
    return np.ascontiguousarray(
        w.reshape(din // 128, 128, dout // 128, 128).transpose(2, 1, 0, 3).reshape(dout // 128, 128, din))


def _img_w13(w):
    wg = w.reshape(4, 2752, DQ)
    wp = np.zeros((4, HPAD, DQ), dtype=w.dtype)
    wp[:, :2752, :] = wg
    wp = wp.reshape(NG, 128, DQ)
    return np.ascontiguousarray(wp.reshape(NG, 128, 8, 128).transpose(0, 3, 2, 1).reshape(NG, 128, DQ))


def _img_w2(w):
    wp = np.zeros((D, HPAD), dtype=w.dtype)
    wp[:, :2752] = w
    return np.ascontiguousarray(
        wp.reshape(32, 128, 22, 128).transpose(0, 3, 2, 1).reshape(32, 128, HPAD))


def _img_a(a):
    n = a.shape[1]
    return np.ascontiguousarray(a.reshape(2, n // 128, 128).transpose(2, 1, 0).reshape(128, (n // 128) * 2))


def _grid_a2(a2):
    ap = np.zeros((2, 4, HPAD), dtype=a2.dtype)
    ap[:, :, :2752] = np.asarray(a2).reshape(2, 4, 2752)
    return _img_a(np.ascontiguousarray(ap.reshape(2, NG * 128)))


def _grid_bT(b):
    bp = np.zeros((4, HPAD, 2), dtype=b.dtype)
    bp[:, :2752, :] = np.asarray(b).reshape(4, 2752, 2)
    return np.ascontiguousarray(bp.reshape(NG * 128, 2).T)




class _SkipBlock:
    def __enter__(self):
        return self

    def __exit__(self, *a):
        return False

    def __getattr__(self, name):
        return lambda f: None

class Sem:
    def __init__(self, h):
        self.h = h
        self.n = 0

    def inc(self, inst, k=1):
        inst.then_inc(self.h, k)
        self.n += k
        return self.n


def _attn_positions(npair):
    scpos = [0] * npair
    ctxpos = [0] * npair
    pos = 1
    scpos[0] = pos
    for p in range(npair):
        if p + 1 < npair:
            pos += 1
            scpos[p + 1] = pos
        pos += 1
        ctxpos[p] = pos
    rbpos = pos + 1
    return scpos, ctxpos, rbpos


PHASE_ORDER = ["setup", "ln0", "x0", "q0", "k0", "v0", "ln1", "x1", "q1", "k1", "v1",
               "attn", "ffn0", "wo0", "out0", "ffn1", "wo1", "out1", "fin"]


def build(taps=False, upto=None):
    lim = len(PHASE_ORDER) if upto is None else PHASE_ORDER.index(upto) + 1

    def on(nm):
        return PHASE_ORDER.index(nm) < lim
    nc = bass.Bass("TRN2", target_bir_lowering=False, debug=False)

    def din(name, shape, dt=F32):
        return nc.dram_tensor(name, shape, dt, kind="ExternalInput").ap()

    x_e = din("x", [T, D])
    wq_e = din("wq_img", [32, 128, D])
    wk_e = din("wk_img", [8, 128, D])
    wv_e = din("wv_img", [8, 128, D])
    wo_e = din("wo_img", [32, 128, D])
    w1_e = din("w1_img", [NG, 128, DQ])
    w3_e = din("w3_img", [NG, 128, DQ])
    w2_e = din("w2_img", [32, 128, HPAD])
    a1_e = din("a1_img", [128, 64])
    a3_e = din("a3_img", [128, 64])
    a2_e = din("a2_img", [128, NG * 2])
    b1_e = din("b1t", [2, NG * 128])
    b3_e = din("b3t", [2, NG * 128])
    b2_e = din("b2t", [2, D])
    cos_e = din("cos2", [128, T])
    sin_e = din("sin2", [128, T])
    onec_e = din("ones_col", [128, 1], BF16)
    oner_e = din("ones_row", [1, 128])
    shift_e = din("shift", [128, 1])
    cmask_e = din("cmask", [128, 2048], BF16)

    y_e = nc.dram_tensor("y", [T, D], F32, kind="ExternalOutput").ap()

    skind = "ExternalOutput" if taps else "Internal"
    xn_h = nc.dram_tensor("xn_hb", [T, D], BF16, kind=skind).ap()
    q_hb = nc.dram_tensor("q_hb", [32, 128, T], BF16, kind=skind).ap()
    vf_hb = nc.dram_tensor("vf_hb", [DQ, T], BF16, kind=skind).ap()
    ctx_hb = nc.dram_tensor("ctx_hb", [D, T], BF16, kind=skind).ap()
    dfm_hb = nc.dram_tensor("dfm_hb", [D, T], BF16, kind=skind).ap()
    xf_hb = nc.dram_tensor("xf_hb", [DQ, T], BF16, kind=skind).ap()
    if taps:
        k_hb = nc.dram_tensor("k_hb", [128, 16384], BF16, kind="ExternalOutput").ap()
        prb_hb = nc.dram_tensor("prb_hb", [128, 1024], BF16, kind="ExternalOutput").ap()

    ab = nc.alloc_sbuf_tensor("A_BF", [128, BF_COLS], BF16)
    af = nc.alloc_sbuf_tensor("A_F32", [128, F32_COLS], F32)
    barena = nc.alloc_sbuf_tensor("A_B", [128, NG * 128], BF16)
    laar = nc.alloc_sbuf_tensor("A_LA", [128, T], BF16)
    a1i = nc.alloc_sbuf_tensor("a1i", [128, 64], BF16)
    a3i = nc.alloc_sbuf_tensor("a3i", [128, 64], BF16)
    a2i = nc.alloc_sbuf_tensor("a2i", [128, NG * 2], BF16)
    onec = nc.alloc_sbuf_tensor("onec", [128, 1], BF16)
    oner = nc.alloc_sbuf_tensor("oner", [1, 128], F32)
    shif = nc.alloc_sbuf_tensor("shif", [128, 1], F32)
    s_sb = nc.alloc_sbuf_tensor("s_sb", [1, 1024], F32)
    r_sb = nc.alloc_sbuf_tensor("r_sb", [1, 1024], F32)
    st1 = nc.alloc_sbuf_tensor("st1", [128, 1], F32)
    st2 = nc.alloc_sbuf_tensor("st2", [128, 1], F32)
    stm = nc.alloc_sbuf_tensor("stm", [128, 1], F32)
    stv = nc.alloc_sbuf_tensor("stv", [128, 1], F32)
    str_ = nc.alloc_sbuf_tensor("str", [128, 1], F32)

    psa = nc.alloc_psum_tensor("PSA", [128, 2048], F32)
    psb = nc.alloc_psum_tensor("PSB", [128, 2048], F32)

    import contextlib
    es = contextlib.ExitStack()
    nc._es = es
    s_ld = [Sem(es.enter_context(nc.semaphore(f"s_ld{i}"))) for i in range(2)]
    s_cast = [Sem(es.enter_context(nc.semaphore(f"s_cast{i}"))) for i in range(2)]
    s_stp = [Sem(es.enter_context(nc.semaphore(f"s_stp{i}"))) for i in range(2)]
    s_dv2 = Sem(es.enter_context(nc.semaphore("s_dv2")))
    s_ac2 = Sem(es.enter_context(nc.semaphore("s_ac2")))
    s_st2 = [Sem(es.enter_context(nc.semaphore(f"s_st2{i}"))) for i in range(2)]
    s_pe = Sem(es.enter_context(nc.semaphore("s_pe")))
    s_dv = Sem(es.enter_context(nc.semaphore("s_dv")))
    s_ac = Sem(es.enter_context(nc.semaphore("s_ac")))
    ALL = [*s_ld, *s_cast, *s_stp, *s_st2, s_dv2, s_ac2, s_pe, s_dv, s_ac]

    def rec(lst, sems, inst):
        sem = sems[len(lst) % 2]
        sem.inc(inst, 16)
        lst.append((sem, sem.n))
        return lst[-1]

    def wt(eng, ent):
        eng.wait_ge(ent[0].h, ent[1])

    def wt_all(eng, lst):
        last = {}
        for sem, val in lst:
            last[id(sem)] = (sem, val)
        for sem, val in last.values():
            eng.wait_ge(sem.h, val)

    def snap():
        return {id(s): s.n for s in ALL}

    def bar(eng, B):
        for s in ALL:
            if B[id(s)] > 0:
                eng.wait_ge(s.h, B[id(s)])

    with nc.Block("setup") as blk:
        @blk.gpsimd
        def _(g):
            L = []
            for sname, dst in [(cos_e, ab[:, C_COS:C_COS + T]),
                               (sin_e, ab[:, C_COS + T:C_COS + 2 * T]),
                               (a1_e, a1i[:]), (a3_e, a3i[:]), (a2_e, a2i[:]),
                               (b1_e, barena[0:2, :]), (b3_e, barena[32:34, :]),
                               (b2_e, barena[64:66, 0:D])]:
                rec(L, s_cast, g.dma_start(out=dst, in_=sname))

        @blk.sync
        def _(sy):
            L = []
            rec(L, s_ld, sy.dma_start(out=onec[:], in_=onec_e))
            rec(L, s_ld, sy.dma_start(out=oner[:], in_=oner_e))
            rec(L, s_ld, sy.dma_start(out=shif[:], in_=shift_e))

    for h in range(2):
        t0 = h * 1024

        B = snap()
        xld, xst = [], []
        with (nc.Block(f"ln{h}") if (on(f"ln{h}") and h == 0) else _SkipBlock()) as blk:
            @blk.sync
            def _(sy, B=B, t0=t0):
                bar(sy, B)
                for t in range(8):
                    if t >= 2:
                        sy.wait_ge(s_dv.h, B[id(s_dv)] + (t - 1) * 8)
                    rec(xld, s_ld, sy.dma_start(out=af[:, (t % 2) * 4096:(t % 2) * 4096 + 4096],
                                                in_=x_e[t0 + t * 128: t0 + (t + 1) * 128, :]))

            @blk.gpsimd
            def _(g, B=B, t0=t0):
                bar(g, B)
                for t in range(8):
                    g.wait_ge(s_dv.h, B[id(s_dv)] + (t + 1) * 8)
                    rec(xst, s_stp, g.dma_start(out=xn_h[t0 + t * 128: t0 + (t + 1) * 128, :],
                                                in_=ab[:, C_LNO + (t % 2) * 4096:C_LNO + (t % 2) * 4096 + 4096]))

            @blk.vector
            def _(v, B=B):
                bar(v, B)
                for t in range(8):
                    xb = af[:, (t % 2) * 4096:(t % 2) * 4096 + 4096]
                    wt(v, xld[t])
                    if t >= 2:
                        wt(v, xst[t - 2])
                    s_dv.inc(v.reduce_sum(st1[:], xb, axis=AX.X))
                    s_dv.inc(v.scalar_tensor_tensor(ab[:, C_LNJ:C_LNJ + 4096], xb, 1.0, xb,
                                                    ALU.mult, ALU.mult, accum_out=st2[:]))
                    v.wait_ge(s_dv.h, s_dv.n)
                    s_dv.inc(v.tensor_scalar_mul(stm[:], st1[:], 1.0 / D))
                    s_dv.inc(v.tensor_scalar_mul(stv[:], st2[:], 1.0 / D))
                    s_dv.inc(v.tensor_mul(str_[:], stm[:], stm[:]))
                    s_dv.inc(v.tensor_sub(stv[:], stv[:], str_[:]))
                    s_dv.inc(v.tensor_scalar_add(stv[:], stv[:], EPS))
                    v.wait_ge(s_ac.h, B[id(s_ac)] + 2 * (t + 1))
                    s_dv.inc(v.tensor_scalar(ab[:, C_LNO + (t % 2) * 4096:C_LNO + (t % 2) * 4096 + 4096],
                                             xb, stm[:], str_[:], ALU.subtract, ALU.mult))

            @blk.scalar
            def _(sc, B=B):
                bar(sc, B)
                for t in range(8):
                    sc.wait_ge(s_dv.h, B[id(s_dv)] + t * 8 + 7)
                    s_ac.inc(sc.activation(st1[:], stv[:], AF.Ln))
                    sc.wait_ge(s_ac.h, s_ac.n)
                    s_ac.inc(sc.activation(str_[:], st1[:], AF.Exp, scale=-0.5))

        B = snap()
        tld, fst = [], []
        with (nc.Block(f"x{h}") if on(f"x{h}") else _SkipBlock()) as blk:
            @blk.sync
            def _(sy, B=B, t0=t0):
                bar(sy, B)
                for c in range(32):
                    rec(tld, s_ld, sy.dma_start(out=ab[:, C_XNF + c * 1024:C_XNF + (c + 1) * 1024],
                                                in_=xn_h[t0:t0 + 1024, c * 128:(c + 1) * 128], transpose=True))

            @blk.gpsimd
            def _(g, B=B, t0=t0):
                bar(g, B)
                for c8 in range(8):
                    g.wait_ge(s_dv.h, B[id(s_dv)] + (c8 + 1) * 3)
                    rec(fst, s_stp, g.dma_start(out=xf_hb[c8 * 128:(c8 + 1) * 128, t0:t0 + 1024],
                                                in_=ab[:, C_XFB + (c8 % 2) * 1024:C_XFB + (c8 % 2) * 1024 + 1024]))

            @blk.vector
            def _(v, B=B, t0=t0):
                bar(v, B)
                xn = lambda c: ab[:, C_XNF + c * 1024:C_XNF + (c + 1) * 1024]
                for c8 in range(8):
                    wt_all(v, tld)
                    t1 = af[:, 0:1024]
                    t2 = af[:, 1024:2048]
                    if c8 >= 2:
                        wt(v, fst[c8 - 2])
                    s_dv.inc(v.tensor_add(t1, xn(c8), xn(8 + c8)))
                    s_dv.inc(v.tensor_add(t2, xn(16 + c8), xn(24 + c8)))
                    s_dv.inc(v.tensor_add(ab[:, C_XFB + (c8 % 2) * 1024:C_XFB + (c8 % 2) * 1024 + 1024], t1, t2))
                v.wait_ge(s_pe.h, B[id(s_pe)] + 1)
                s_dv.inc(v.tensor_scalar_mul(laar[0:2, t0:t0 + 1024], psb[0:2, 0:1024], 0.5))
                v.wait_ge(s_pe.h, B[id(s_pe)] + 2)
                s_dv.inc(v.tensor_scalar_mul(laar[32:34, t0:t0 + 1024], psb[0:2, 1024:2048], 0.5))

            @blk.tensor
            def _(te, B=B):
                bar(te, B)
                wt_all(te, tld)
                for which, img in ((0, a1i), (1, a3i)):
                    ps = psb[0:2, 0:1024] if which == 0 else psb[0:2, 1024:2048]
                    mm = None
                    for c in range(32):
                        for n in range(2):
                            mm = te.matmul(ps[:, n * 512:(n + 1) * 512], img[:, c * 2:c * 2 + 2],
                                           ab[:, C_XNF + c * 1024 + n * 512:C_XNF + c * 1024 + n * 512 + 512],
                                           start=(c == 0), stop=(c == 31))
                    s_pe.inc(mm)

        for proj, no, img_e, rope in (("q", 32, wq_e, True), ("k", 8, wk_e, True), ("v", 8, wv_e, False)):
            B = snap()
            sst = []
            gln = (proj == "q" and h == 0)
            gxld, gxst = [], []
            LB = dict(dv2=s_dv2.n, ac2=s_ac2.n)

            def gln_stats(v, t):
                wt(v, gxld[t])
                if t >= 2:
                    wt(v, gxst[t - 2])
                xb = af[:, (t % 2) * 4096:(t % 2) * 4096 + 4096]
                s_dv2.inc(v.reduce_sum(st1[:], xb, axis=AX.X))
                s_dv2.inc(v.scalar_tensor_tensor(ab[:, C_K:C_K + 4096], xb, 1.0, xb,
                                                 ALU.mult, ALU.mult, accum_out=st2[:]))
                v.wait_ge(s_dv2.h, s_dv2.n)
                v.tensor_scalar_mul(stm[:], st1[:], 1.0 / D)
                v.tensor_mul(str_[:], stm[:], stm[:])
                v.tensor_scalar_mul(stv[:], st2[:], 1.0 / D)
                v.tensor_sub(stv[:], stv[:], str_[:])
                s_dv2.inc(v.tensor_scalar_add(stv[:], stv[:], EPS))

            def gln_norm(v, t):
                v.wait_ge(s_ac2.h, LB["ac2"] + 2 * (t + 1))
                if t >= 2:
                    wt(v, gxst[t - 2])
                xb = af[:, (t % 2) * 4096:(t % 2) * 4096 + 4096]
                nb = ab[:, C_K + 4096 + (t % 2) * 4096:C_K + 8192 + (t % 2) * 4096]
                s_dv2.inc(v.tensor_scalar(nb, xb, stm[:], str_[:], ALU.subtract, ALU.mult))

            def gln_act(sc, t):
                sc.wait_ge(s_dv2.h, LB["dv2"] + 4 * t + 3)
                s_ac2.inc(sc.activation(st1[:], stv[:], AF.Ln))
                sc.wait_ge(s_ac2.h, s_ac2.n)
                s_ac2.inc(sc.activation(str_[:], st1[:], AF.Exp, scale=-0.5))

            def gln_store(sy, t, B):
                sy.wait_ge(s_dv2.h, LB["dv2"] + 4 * (t + 1))
                sem2 = s_st2[t % 2]
                if t >= 2:
                    sy.wait_ge(sem2.h, B[id(sem2)] + 16 * (t // 2))
                sem2.inc(sy.dma_start(out=xn_h[1024 + t * 128:1024 + (t + 1) * 128, :],
                                      in_=ab[:, C_K + 4096 + (t % 2) * 4096:C_K + 8192 + (t % 2) * 4096]), 16)
                gxst.append((sem2, sem2.n))
            wld = {}
            with (nc.Block(f"{proj}{h}") if on(f"{proj}{h}") else _SkipBlock()) as blk:
                @blk.gpsimd
                def _(g, B=B, no=no, img_e=img_e):
                    bar(g, B)
                    for o in range(no):
                        if o >= 2:
                            g.wait_ge(s_pe.h, B[id(s_pe)] + o - 1)
                        sem = s_cast[o % 2]
                        base = C_WQ + (o % 2) * 4096
                        sem.inc(g.dma_start(out=ab[0:64, base:base + 4096], in_=img_e[o][0:64]), 16)
                        sem.inc(g.dma_start(out=ab[64:128, base:base + 4096], in_=img_e[o][64:128]), 16)
                        wld[o] = (sem, sem.n)

                @blk.sync
                def _(sy, B=B, no=no, proj=proj, t0=t0):
                    bar(sy, B)
                    if gln:
                        for t in range(8):
                            if t >= 2:
                                gln_store(sy, t - 2, B)
                            sem = s_ld[t % 2]
                            sem.inc(sy.dma_start(out=af[:, (t % 2) * 4096:(t % 2) * 4096 + 4096],
                                                 in_=x_e[1024 + t * 128:1024 + (t + 1) * 128, :]), 16)
                            gxld.append((sem, sem.n))
                        gln_store(sy, 6, B)
                        gln_store(sy, 7, B)
                    if proj == "v":
                        for o in range(no):
                            sb = ab[:, C_QEV + (o % 4) * 1024:C_QEV + (o % 4) * 1024 + 1024]
                            sy.wait_ge(s_ac.h, B[id(s_ac)] + o + 1)
                            rec(sst, s_stp, sy.dma_start(out=vf_hb[o * 128:(o + 1) * 128, t0:t0 + 1024], in_=sb))

                @blk.scalar
                def _(sc, B=B, no=no, rope=rope, proj=proj, t0=t0):
                    bar(sc, B)
                    for o in range(no):
                        if proj == "q" and o >= 1:
                            sc.wait_ge(s_dv.h, B[id(s_dv)] + o * 5)
                            rec(sst, s_stp, sc.dma_start(out=q_hb[o - 1][:, t0:t0 + 1024],
                                                         in_=ab[:, C_SPILL + ((o - 1) % 2) * 1024:C_SPILL + ((o - 1) % 2) * 1024 + 1024]))
                        if gln and o % 4 == 3:
                            gln_act(sc, o // 4)
                        sc.wait_ge(s_pe.h, B[id(s_pe)] + o + 1)
                        ps = psa[:, (o % 2) * 1024:(o % 2) * 1024 + 1024]
                        if rope:
                            if o >= 2:
                                sc.wait_ge(s_dv.h, B[id(s_dv)] + (o - 1) * 5)
                            s_ac.inc(sc.copy(ab[:, C_QEV + (o % 2) * 1024:C_QEV + (o % 2) * 1024 + 1024], ps))
                        else:
                            if o >= 4:
                                wt(sc, sst[o - 4])
                            s_ac.inc(sc.copy(ab[:, C_QEV + (o % 4) * 1024:C_QEV + (o % 4) * 1024 + 1024], ps))
                    if proj == "q":
                        sc.wait_ge(s_dv.h, B[id(s_dv)] + no * 5)
                        rec(sst, s_stp, sc.dma_start(out=q_hb[no - 1][:, t0:t0 + 1024],
                                                     in_=ab[:, C_SPILL + ((no - 1) % 2) * 1024:C_SPILL + ((no - 1) % 2) * 1024 + 1024]))

                @blk.tensor
                def _(te, B=B, no=no):
                    bar(te, B)
                    for o in range(no):
                        wt(te, wld[o])
                        if o >= 2:
                            te.wait_ge(s_ac.h, B[id(s_ac)] + o - 1)
                        ps = psa[:, (o % 2) * 1024:(o % 2) * 1024 + 1024]
                        wb = ab[:, C_WQ + (o % 2) * 4096:C_WQ + (o % 2) * 4096 + 4096]
                        mm = None
                        for d in range(32):
                            for n in range(2):
                                mm = te.matmul(ps[:, n * 512:(n + 1) * 512], wb[:, d * 128:(d + 1) * 128],
                                               ab[:, C_XNF + d * 1024 + n * 512:C_XNF + d * 1024 + n * 512 + 512],
                                               start=(d == 0), stop=(d == 31))
                        s_pe.inc(mm)

                if rope:
                    @blk.vector
                    def _(v, B=B, no=no, proj=proj, t0=t0):
                        bar(v, B)
                        cosb = ab[:, C_COS + t0:C_COS + t0 + 1024]
                        sinb = ab[:, C_COS + T + t0:C_COS + T + t0 + 1024]
                        for o in range(no):
                            if gln and o % 4 == 1:
                                gln_stats(v, o // 4)
                            if gln and o % 4 == 3:
                                gln_norm(v, o // 4)
                            v.wait_ge(s_ac.h, B[id(s_ac)] + o + 1)
                            qe = ab[:, C_QEV + (o % 2) * 1024:C_QEV + (o % 2) * 1024 + 1024]
                            m1 = ab[:, C_M1:C_M1 + 1024]
                            qsw = ab[:, C_M2:C_M2 + 1024]
                            if proj == "q":
                                dst = ab[:, C_SPILL + (o % 2) * 1024:C_SPILL + (o % 2) * 1024 + 1024]
                                if o >= 2:
                                    wt(v, sst[o - 2])
                            else:
                                dst = ab[:, C_K + o * 2048 + t0:C_K + o * 2048 + t0 + 1024]
                            s_dv.inc(v.tensor_copy(qsw[0:64, :], qe[64:128, :]))
                            s_dv.inc(v.tensor_copy(qsw[64:128, :], qe[0:64, :]))
                            s_dv.inc(v.tensor_mul(m1, qe, cosb))
                            s_dv.inc(v.tensor_mul(qe, qsw, sinb))
                            s_dv.inc(v.tensor_add(dst, m1, qe))

    B = snap()
    NKT = [4, 8, 12, 16]

    def np_of(it):
        return NKT[it % 4] // 2

    def pe_events():
        for j in range(64):
            A, Bv = 2 * j, 2 * j + 1
            np_A, np_B = np_of(A), np_of(Bv)
            if j == 0:
                yield ("sc", A, 0)
                yield ("sc", Bv, 0)
            for p in range(np_B):
                for it in (A, Bv):
                    if p < np_of(it):
                        if p + 1 < np_of(it):
                            yield ("sc", it, p + 1)
                        yield ("sctx", it, p)
                if p == np_A - 1 and A + 2 < 128:
                    yield ("sc", A + 2, 0)
                if p == np_A:
                    yield ("rb", A, 0)
            if Bv + 2 < 128:
                yield ("sc", Bv + 2, 0)
            yield ("rb", Bv, 0)

    def ac_events():
        for j in range(64):
            A, Bv = 2 * j, 2 * j + 1
            np_A, np_B = np_of(A), np_of(Bv)
            if j == 0:
                yield ("exp", A, 0)
                yield ("exp", Bv, 0)
            for p in range(1, np_B):
                for it in (A, Bv):
                    if p < np_of(it):
                        yield ("exp", it, p)
                if p == np_A - 1:
                    yield ("rcp", A, 0)
                if p == np_A and A + 2 < 128:
                    yield ("exp", A + 2, 0)
            yield ("rcp", Bv, 0)
            if Bv + 2 < 128:
                yield ("exp", Bv + 2, 0)

    def dv_events():
        for j in range(64):
            A, Bv = 2 * j, 2 * j + 1
            yield ("m", A, 0)
            yield ("m", A, 1)
            yield ("m", Bv, 0)
            yield ("m", Bv, 1)
            yield ("rbc", A, 0)
            yield ("tt", A, 0)
            yield ("rbc", Bv, 0)
            yield ("tt", Bv, 0)

    pe_pos, ac_pos, dv_pos = {}, {}, {}
    n = B[id(s_pe)]
    for ev in pe_events():
        n += 1
        pe_pos[ev] = n
    n = B[id(s_ac)]
    for ev in ac_events():
        n += 2 if ev[0] == "rcp" else 1
        ac_pos[ev] = n
    n = B[id(s_dv)]
    for ev in dv_events():
        n += 1
        dv_pos[ev] = n
    vld, qld, cst = [], [], []

    def scbuf(it):
        return psa[:, (it % 2) * 1024:(it % 2) * 1024 + 1024]

    def prbuf(it):
        return ab[:, C_PRB + (it % 2) * 1024:C_PRB + (it % 2) * 1024 + 1024]

    def ctxbuf(it):
        return psb[:, (it % 2) * 512:(it % 2) * 512 + 512]

    def srbbuf(it):
        return psb[:, 1024 + (it % 2) * 512:1024 + (it % 2) * 512 + 512]

    def qsbuf(it):
        return ab[:, 22528 + (it % 4) * 512:22528 + (it % 4) * 512 + 512]

    def rbsb(it):
        return af[:, 6144 + (it % 2) * 512:6144 + (it % 2) * 512 + 512]

    def ctxo(it):
        return ab[:, C_CTXO + (it % 2) * 512:C_CTXO + (it % 2) * 512 + 512]

    with (nc.Block("attn") if on("attn") else _SkipBlock()) as blk:
        @blk.sync
        def _(sy, B=B):
            bar(sy, B)
            rec(vld, s_ld, sy.dma_start(out=ab[:, C_MSK:C_MSK + 2048], in_=cmask_e))
            for kt in range(16):
                rec(vld, s_ld, sy.dma_start(out=ab[:, C_V + kt * 1024:C_V + (kt + 1) * 1024],
                                            in_=vf_hb[:, kt * 128:(kt + 1) * 128], transpose=True))
            wt_all(sy, vld)
            QSEM = [s_ld[0], s_ld[1], s_cast[0], s_cast[1]]
            for it in range(128):
                hq, qb = it // 4, it % 4
                if it >= 4:
                    sy.wait_ge(s_pe.h, pe_pos[("sc", it - 4, np_of(it - 4) - 1)])
                sem = QSEM[it % 4]
                sem.inc(sy.dma_start(out=qsbuf(it), in_=q_hb[hq][:, qb * 512:(qb + 1) * 512]), 16)
                qld.append((sem, sem.n))

        @blk.tensor
        def _(te, B=B):
            bar(te, B)
            wt_all(te, vld)
            qwaited = set()
            for ev in pe_events():
                tag, it, p = ev
                g = (it // 4) // 4
                nkt = NKT[it % 4]
                if tag == "sc":
                    if it not in qwaited:
                        wt(te, qld[it])
                        qwaited.add(it)
                    if p == 0 and it >= 2:
                        te.wait_ge(s_ac.h, ac_pos[("exp", it - 2, np_of(it - 2) - 1)])
                    if p >= 1:
                        te.wait_ge(s_ac.h, ac_pos[("exp", it, p - 1)])
                    scps = scbuf(it)
                    mm = None
                    for jj in range(2):
                        kt = 2 * p + jj
                        klhs = ab[:, C_K + g * 2048 + kt * 128:C_K + g * 2048 + kt * 128 + 128]
                        mm = te.matmul(scps[:, jj * 512:jj * 512 + 512], klhs, qsbuf(it), start=True, stop=True)
                    s_pe.inc(mm)
                elif tag == "rb":
                    te.wait_ge(s_ac.h, ac_pos[("rcp", it, 0)])
                    s_pe.inc(te.matmul(srbbuf(it), oner[:], r_sb[0:1, (it % 2) * 512:(it % 2) * 512 + 512],
                                       start=True, stop=True))
                elif tag == "sctx":
                    te.wait_ge(s_ac.h, ac_pos[("exp", it, p)])
                    npair = nkt // 2
                    if p >= npair - 2:
                        te.wait_ge(s_dv.h, dv_pos[("m", it, p - (npair - 2))])
                    if p == 0 and it >= 2:
                        te.wait_ge(s_dv.h, dv_pos[("tt", it - 2, 0)])
                    prb = prbuf(it)
                    mm = None
                    for jj in range(2):
                        kt = 2 * p + jj
                        vl = ab[:, C_V + kt * 1024 + g * 128:C_V + kt * 1024 + g * 128 + 128]
                        te.matmul(srbbuf(it)[0:1, :], onec[:], prb[:, jj * 512:jj * 512 + 512],
                                  start=(kt == 0), stop=(kt == nkt - 1))
                        mm = te.matmul(ctxbuf(it), vl, prb[:, jj * 512:jj * 512 + 512],
                                       start=(kt == 0), stop=(kt == nkt - 1))
                    s_pe.inc(mm)

        @blk.scalar
        def _(sc, B=B):
            bar(sc, B)
            for ev in ac_events():
                tag, it, p = ev
                if tag == "rcp":
                    sc.wait_ge(s_pe.h, pe_pos[("sctx", it, np_of(it) - 1)])
                    if it >= 2:
                        sc.wait_ge(s_pe.h, pe_pos[("rb", it - 2, 0)])
                    sslot = s_sb[0:1, (it % 2) * 512:(it % 2) * 512 + 512]
                    s_ac.inc(sc.activation(sslot, srbbuf(it)[0:1, :], AF.Ln))
                    sc.wait_ge(s_ac.h, s_ac.n)
                    s_ac.inc(sc.activation(r_sb[0:1, (it % 2) * 512:(it % 2) * 512 + 512],
                                           sslot, AF.Exp, scale=-1.0))
                    continue
                if p >= 1:
                    sc.wait_ge(s_pe.h, pe_pos[("sctx", it, p - 1)])
                else:
                    sc.wait_ge(s_pe.h, pe_pos[("sc", it, 0)])
                    if it >= 2:
                        sc.wait_ge(s_pe.h, pe_pos[("sctx", it - 2, np_of(it - 2) - 1)])
                s_ac.inc(sc.activation(prbuf(it), scbuf(it), AF.Exp, bias=shif[:], scale=ISQ))

        @blk.gpsimd
        def _(g, B=B):
            bar(g, B)
            if taps:
                g.wait_ge(s_dv.h, B[id(s_dv)])
                kst = []
                rec(kst, s_stp, g.dma_start(out=k_hb, in_=ab[:, C_K:C_K + 16384]))
                g.wait_ge(s_dv.h, dv_pos[("m", 0, 0)])
                rec(kst, s_stp, g.dma_start(out=prb_hb, in_=ab[:, C_PRB:C_PRB + 1024]))
            for it in range(128):
                hq, qb = it // 4, it % 4
                g.wait_ge(s_dv.h, dv_pos[("tt", it, 0)])
                rec(cst, s_stp, g.dma_start(out=ctx_hb[hq * 128:(hq + 1) * 128, qb * 512:(qb + 1) * 512],
                                            in_=ctxo(it)), )

        @blk.vector
        def _(v, B=B):
            bar(v, B)
            for ev in dv_events():
                tag, it, di = ev
                npair = np_of(it)
                if tag == "rbc":
                    v.wait_ge(s_pe.h, pe_pos[("rb", it, 0)])
                    s_dv.inc(v.tensor_copy(rbsb(it), srbbuf(it)[:]))
                elif tag == "tt":
                    if it >= 2:
                        wt(v, cst[it - 2])
                    s_dv.inc(v.tensor_mul(ctxo(it), ctxbuf(it)[:], rbsb(it)))
                elif tag == "m":
                    p = npair - 2 + di
                    v.wait_ge(s_ac.h, ac_pos[("exp", it, p)])
                    prb = prbuf(it)
                    mm = None
                    for jj in range(2):
                        mi = 2 * di + jj
                        mm = v.tensor_mul(prb[:, jj * 512:jj * 512 + 512],
                                          prb[:, jj * 512:jj * 512 + 512],
                                          ab[:, C_MSK + mi * 512:C_MSK + mi * 512 + 512])
                    s_dv.inc(mm)

    B_pre_out = [None]
    for h in range(2):
        t0 = h * 1024

        B = snap()
        NU = 176
        BPO = B_pre_out[0]
        xfld, w13ld, w2ld, dst_ = [], [], {}, []
        ghost = (h == 1)
        go_ld, go_st = [], []
        GB = dict(dv2=s_dv2.n)

        def ffn_entry(eng, B=B, BPO=BPO, h=h):
            import os as _os
            if h == 1 and BPO is not None and _os.environ.get("NO_OVL", "0") != "1":
                for s, key in ((s_pe, id(s_pe)), (s_dv, id(s_dv)), (s_ac, id(s_ac))):
                    if BPO[key] > 0:
                        eng.wait_ge(s.h, BPO[key])
                for s in (*s_stp, *s_cast):
                    if BPO[id(s)] > 0:
                        eng.wait_ge(s.h, BPO[id(s)])
            else:
                bar(eng, B)
        with (nc.Block(f"ffn{h}") if on(f"ffn{h}") else _SkipBlock()) as blk:
            @blk.gpsimd
            def _(g, B=B, t0=t0):
                ffn_entry(g)
                for idx in range(NG):
                    if idx >= 2:
                        g.wait_ge(s_pe.h, B[id(s_pe)] + 4 * idx - 6)
                    sem = s_cast[idx % 2]
                    sem.inc(g.dma_start(out=ab[:, C_W1 + (idx % 2) * 1024:C_W1 + (idx % 2) * 1024 + 1024],
                                        in_=w1_e[(idx % 4) * 22 + idx // 4]), 16)
                    sem.inc(g.dma_start(out=ab[:, C_W3 + (idx % 2) * 1024:C_W3 + (idx % 2) * 1024 + 1024],
                                        in_=w3_e[(idx % 4) * 22 + idx // 4]), 16)
                    w13ld.append([(sem, sem.n)])
                for o in range(32):
                    if o >= 2:
                        g.wait_ge(s_pe.h, B[id(s_pe)] + 2 * NU + o - 1)
                    sem = s_cast[o % 2]
                    base = C_W2 + (o % 2) * HPAD
                    sem.inc(g.dma_start(out=ab[0:64, base:base + HPAD], in_=w2_e[o][0:64]), 16)
                    sem.inc(g.dma_start(out=ab[64:128, base:base + HPAD], in_=w2_e[o][64:128]), 16)
                    w2ld[o] = (sem, sem.n)

            @blk.scalar
            def _(sc, B=B):
                ffn_entry(sc)
                def ystore(t):
                    sc.wait_ge(s_dv2.h, GB["dv2"] + 4 * (t + 1))
                    sem2 = s_st2[t % 2]
                    sem2.inc(sc.dma_start(out=y_e[t * 128:(t + 1) * 128, :],
                                          in_=af[:, (t % 2) * 4096:(t % 2) * 4096 + 4096]), 16)
                    go_st.append((sem2, sem2.n))
                pe_a = lambda u: (2 * u) if u >= 1 else 1
                for u in range(NU):
                    if ghost and u % 22 == 0 and u >= 22:
                        ystore(u // 22 - 1)
                    upar = u % 2
                    if u >= 2:
                        sc.wait_ge(s_dv.h, B[id(s_dv)] + u - 1)
                    sc.wait_ge(s_pe.h, B[id(s_pe)] + pe_a(u))
                    s_ac.inc(sc.activation(ab[:, C_GATE + upar * 512:C_GATE + upar * 512 + 512],
                                           psa[:, upar * 512:upar * 512 + 512], AF.Silu))
                if ghost:
                    ystore(7)
                for o in range(32):
                    sc.wait_ge(s_pe.h, B[id(s_pe)] + 2 * NU + o + 1)
                    if o >= 2:
                        wt(sc, dst_[o - 2])
                    s_ac.inc(sc.copy(ab[:, C_DEV + (o % 2) * 1024:C_DEV + (o % 2) * 1024 + 1024],
                                     psa[:, (o % 2) * 1024:(o % 2) * 1024 + 1024]))
                    rec(dst_, s_stp, sc.dma_start(out=dfm_hb[o * 128:(o + 1) * 128, t0:t0 + 1024],
                                                  in_=ab[:, C_DEV + (o % 2) * 1024:C_DEV + (o % 2) * 1024 + 1024]))

            @blk.sync
            def _(sy, B=B, t0=t0):
                ffn_entry(sy)
                for c8 in range(8):
                    rec(xfld, s_ld, sy.dma_start(out=ab[:, C_XF + c8 * 1024:C_XF + (c8 + 1) * 1024],
                                                 in_=xf_hb[c8 * 128:(c8 + 1) * 128, t0:t0 + 1024]))
                if ghost:
                    wt_all(sy, xfld)
                    for t in range(8):
                        if t >= 2:
                            sy.wait_ge(s_dv2.h, GB["dv2"] + 4 * (t - 1))
                        sem = s_ld[t % 2]
                        sem.inc(sy.dma_start(out=ab[:, C_DTM + (t % 2) * 4096:C_DTM + (t % 2) * 4096 + 4096],
                                             in_=dfm_hb[:, t * 128:(t + 1) * 128], transpose=True), 16)
                        if t >= 2:
                            wt(sy, go_st[t - 2])
                        sem.inc(sy.dma_start(out=af[:, (t % 2) * 4096:(t % 2) * 4096 + 4096],
                                             in_=x_e[t * 128:(t + 1) * 128, :]), 16)
                        go_ld.append((sem, sem.n))

            @blk.tensor
            def _(te, B=B, t0=t0):
                ffn_entry(te)
                wt_all(te, xfld)
                for u in range(NU):
                    idx, tb = u // 2, u % 2
                    cp, r = idx // 4, idx % 4
                    upar = u % 2
                    b1ps = psa[:, upar * 512:upar * 512 + 512]
                    b3ps = psa[:, 1024 + upar * 512:1024 + upar * 512 + 512]
                    for ent in w13ld[idx]:
                        wt(te, ent)
                    if u >= 2:
                        te.wait_ge(s_ac.h, B[id(s_ac)] + u - 1)
                        te.wait_ge(s_dv.h, B[id(s_dv)] + u - 1)
                    w1b = ab[:, C_W1 + (idx % 2) * 1024:C_W1 + (idx % 2) * 1024 + 1024]
                    w3b = ab[:, C_W3 + (idx % 2) * 1024:C_W3 + (idx % 2) * 1024 + 1024]
                    g_ = r * 22 + cp
                    mm = None
                    for wb, bps, brow in ((w1b, b1ps, 0), (w3b, b3ps, 32)):
                        for d in range(8):
                            mm = te.matmul(bps, wb[:, d * 128:(d + 1) * 128],
                                           ab[:, C_XF + d * 1024 + tb * 512:C_XF + d * 1024 + tb * 512 + 512],
                                           start=(d == 0), stop=False)
                        mm = te.matmul(bps, barena[brow:brow + 2, g_ * 128:(g_ + 1) * 128],
                                       laar[brow:brow + 2, t0 + tb * 512:t0 + tb * 512 + 512],
                                       start=False, stop=True)
                    s_pe.inc(mm)
                    if u >= 1:
                        um = u - 1
                        idxm, tbm = um // 2, um % 2
                        gm = (idxm % 4) * 22 + idxm // 4
                        te.wait_ge(s_dv.h, B[id(s_dv)] + um + 1)
                        s_pe.inc(te.matmul(psb[0:2, tbm * 512:tbm * 512 + 512],
                                           a2i[:, gm * 2:gm * 2 + 2],
                                           ab[:, C_YT + (um % 2) * 512:C_YT + (um % 2) * 512 + 512],
                                           start=(idxm == 0), stop=(idxm == NG - 1)))
                um = NU - 1
                idxm, tbm = um // 2, um % 2
                gm = (idxm % 4) * 22 + idxm // 4
                te.wait_ge(s_dv.h, B[id(s_dv)] + um + 1)
                s_pe.inc(te.matmul(psb[0:2, tbm * 512:tbm * 512 + 512],
                                   a2i[:, gm * 2:gm * 2 + 2],
                                   ab[:, C_YT + (um % 2) * 512:C_YT + (um % 2) * 512 + 512],
                                   start=(idxm == 0), stop=(idxm == NG - 1)))
                for o in range(32):
                    wt(te, w2ld[o])
                    if o >= 2:
                        te.wait_ge(s_ac.h, B[id(s_ac)] + NU + o - 1)
                    if o == 0:
                        te.wait_ge(s_dv.h, B[id(s_dv)] + NU + 1)
                    ps = psa[:, (o % 2) * 1024:(o % 2) * 1024 + 1024]
                    w2b = ab[:, C_W2 + (o % 2) * HPAD:C_W2 + (o % 2) * HPAD + HPAD]
                    mm = None
                    for n in range(2):
                        for cp in range(22):
                            mm = te.matmul(ps[:, n * 512:n * 512 + 512], w2b[:, cp * 128:(cp + 1) * 128],
                                           ab[:, C_YF + cp * 1024 + n * 512:C_YF + cp * 1024 + n * 512 + 512],
                                           start=(cp == 0), stop=False)
                        mm = te.matmul(ps[:, n * 512:n * 512 + 512], barena[64:66, o * 128:(o + 1) * 128],
                                       laar[64:66, t0 + n * 512:t0 + n * 512 + 512],
                                       start=False, stop=True)
                    s_pe.inc(mm)

            @blk.vector
            def _(v, B=B, t0=t0):
                ffn_entry(v)
                for u in range(NU):
                    if ghost and (u % 22) in (6, 11, 16, 21) and u // 22 < 8:
                        t = u // 22
                        c = (6, 11, 16, 21).index(u % 22)
                        if c == 0:
                            wt(v, go_ld[t])
                        xb2 = af[:, (t % 2) * 4096 + c * 1024:(t % 2) * 4096 + c * 1024 + 1024]
                        s_dv2.inc(v.tensor_add(xb2, xb2,
                                               ab[:, C_DTM + (t % 2) * 4096 + c * 1024:C_DTM + (t % 2) * 4096 + c * 1024 + 1024]))
                    idx, tb = u // 2, u % 2
                    r = idx % 4
                    cp = idx // 4
                    upar = u % 2
                    v.wait_ge(s_ac.h, B[id(s_ac)] + u + 1)
                    gate = ab[:, C_GATE + upar * 512:C_GATE + upar * 512 + 512]
                    b3ps = psa[:, 1024 + upar * 512:1024 + upar * 512 + 512]
                    yt = ab[:, C_YT + upar * 512:C_YT + upar * 512 + 512]
                    yac = af[:, 8192 + tb * 512:8192 + tb * 512 + 512]
                    if u >= 2:
                        v.wait_ge(s_pe.h, B[id(s_pe)] + 2 * (u - 2) + 3)
                    v.tensor_mul(yt, gate, b3ps)
                    if r == 0:
                        s_dv.inc(v.tensor_mul(yac, gate, b3ps))
                    elif r < 3:
                        s_dv.inc(v.tensor_add(yac, yac, yt))
                    else:
                        s_dv.inc(v.tensor_add(ab[:, C_YF + cp * 1024 + tb * 512:C_YF + cp * 1024 + tb * 512 + 512],
                                              yac, yt))
                v.wait_ge(s_pe.h, B[id(s_pe)] + 2 * NU)
                s_dv.inc(v.tensor_scalar_mul(laar[64:66, t0:t0 + 1024], psb[0:2, 0:1024], 0.5))

        B = snap()
        cld, dld, wst = [], [], []
        wold = {}
        gwo = (h == 1)
        gp_ld = []
        GB2 = s_dv2.n
        AC2 = s_ac2.n
        with (nc.Block(f"wo{h}") if on(f"wo{h}") else _SkipBlock()) as blk:
            @blk.gpsimd
            def _(g, B=B, t0=t0):
                bar(g, B)
                def wstore(o):
                    g.wait_ge(s_dv.h, B[id(s_dv)] + o + 1)
                    rec(wst, s_stp, g.dma_start(out=dfm_hb[o * 128:(o + 1) * 128, t0:t0 + 1024],
                                                in_=ab[:, C_DEV2 + (o % 2) * 1024:C_DEV2 + (o % 2) * 1024 + 1024]))
                for o in range(32):
                    if o >= 2:
                        g.wait_ge(s_pe.h, B[id(s_pe)] + o - 1)
                    sem = s_cast[o % 2]
                    base = C_WO + (o % 2) * 4096
                    sem.inc(g.dma_start(out=ab[0:64, base:base + 4096], in_=wo_e[o][0:64]), 16)
                    sem.inc(g.dma_start(out=ab[64:128, base:base + 4096], in_=wo_e[o][64:128]), 16)
                    wold[o] = (sem, sem.n)
                    if o >= 1:
                        wstore(o - 1)
                wstore(31)

            @blk.scalar
            def _(sc, B=B, t0=t0):
                bar(sc, B)
                def ystore(t, hf):
                    sc.wait_ge(s_dv2.h, GB2 + 8 * hf + t + 1)
                    sc.wait_ge(s_ac2.h, AC2 + 128 * hf + 16 * t)
                    s_ac2.inc(sc.dma_start(out=y_e[t0 + t * 128:t0 + (t + 1) * 128, hf * 2048:hf * 2048 + 2048],
                                           in_=af[:, (t % 4) * 2048:(t % 4) * 2048 + 2048]), 16)
                if gwo:
                    for t in range(8):
                        ystore(t, 0)
                    for t in range(8):
                        ystore(t, 1)

            @blk.sync
            def _(sy, B=B, t0=t0):
                bar(sy, B)
                for c in range(32):
                    rec(cld, s_ld, sy.dma_start(out=ab[:, C_CTX + c * 1024:C_CTX + (c + 1) * 1024],
                                                in_=ctx_hb[c * 128:(c + 1) * 128, t0:t0 + 1024]))
                wt_all(sy, cld)
                def gpair(t, hf):
                    if hf == 0:
                        if t >= 4:
                            sy.wait_ge(s_ac2.h, AC2 + 16 * (t - 3))
                    else:
                        sy.wait_ge(s_ac2.h, AC2 + 128 + (16 * (t - 3) if t >= 4 else 0))
                    sem = s_st2[t % 2]
                    sem.inc(sy.dma_start(out=ab[:, C_DTM + (t % 4) * 2048:C_DTM + (t % 4) * 2048 + 2048],
                                         in_=dfm_hb[hf * 2048:hf * 2048 + 2048, t0 + t * 128:t0 + (t + 1) * 128],
                                         transpose=True), 16)
                    sem.inc(sy.dma_start(out=af[:, (t % 4) * 2048:(t % 4) * 2048 + 2048],
                                         in_=x_e[t0 + t * 128:t0 + (t + 1) * 128, hf * 2048:hf * 2048 + 2048]), 16)
                    gp_ld.append((sem, sem.n))
                for o in range(32):
                    if o >= 2:
                        sy.wait_ge(s_dv.h, B[id(s_dv)] + o - 1)
                    rec(dld, s_ld, sy.dma_start(out=ab[:, C_DBK + (o % 2) * 1024:C_DBK + (o % 2) * 1024 + 1024],
                                                in_=dfm_hb[o * 128:(o + 1) * 128, t0:t0 + 1024]))
                    if gwo and 19 <= o < 27:
                        if o == 19:
                            wt_all(sy, wst[:16])
                        gpair(o - 19, 0)
                if gwo:
                    wt_all(sy, wst)
                    for t in range(8):
                        gpair(t, 1)

            @blk.tensor
            def _(te, B=B):
                bar(te, B)
                wt_all(te, cld)
                for o in range(32):
                    wt(te, wold[o])
                    if o >= 2:
                        te.wait_ge(s_dv.h, B[id(s_dv)] + o - 1)
                    ps = psa[:, (o % 2) * 1024:(o % 2) * 1024 + 1024]
                    wb = ab[:, C_WO + (o % 2) * 4096:C_WO + (o % 2) * 4096 + 4096]
                    mm = None
                    for d in range(32):
                        for n in range(2):
                            mm = te.matmul(ps[:, n * 512:(n + 1) * 512], wb[:, d * 128:(d + 1) * 128],
                                           ab[:, C_CTX + d * 1024 + n * 512:C_CTX + d * 1024 + n * 512 + 512],
                                           start=(d == 0), stop=(d == 31))
                    s_pe.inc(mm)

            @blk.vector
            def _(v, B=B):
                bar(v, B)
                def gadd(t, hf):
                    wt(v, gp_ld[8 * hf + t])
                    xh = af[:, (t % 4) * 2048:(t % 4) * 2048 + 2048]
                    s_dv2.inc(v.tensor_add(xh, xh,
                                           ab[:, C_DTM + (t % 4) * 2048:C_DTM + (t % 4) * 2048 + 2048]))
                for o in range(32):
                    v.wait_ge(s_pe.h, B[id(s_pe)] + o + 1)
                    wt(v, dld[o])
                    if o >= 2:
                        wt(v, wst[o - 2])
                    s_dv.inc(v.tensor_add(ab[:, C_DEV2 + (o % 2) * 1024:C_DEV2 + (o % 2) * 1024 + 1024],
                                          psa[:, (o % 2) * 1024:(o % 2) * 1024 + 1024],
                                          ab[:, C_DBK + (o % 2) * 1024:C_DBK + (o % 2) * 1024 + 1024]))
                    if gwo and 21 <= o < 29:
                        gadd(o - 21, 0)
                if gwo:
                    for t in range(8):
                        gadd(t, 1)

        B = snap()
        B_pre_out[0] = dict(B)
        old_, yst, shalf = [], [], []
        OUT_SKIP = True
        with (nc.Block(f"out{h}") if (on(f"out{h}") and not OUT_SKIP) else _SkipBlock()) as blk:
            @blk.gpsimd
            def _(g, B=B, t0=t0):
                bar(g, B)
                for t in range(8):
                    g.wait_ge(s_dv.h, B[id(s_dv)] + t + 1)
                    rec(yst, s_stp, g.dma_start(out=y_e[t0 + t * 128:t0 + (t + 1) * 128, :],
                                                in_=af[:, (t % 2) * 4096:(t % 2) * 4096 + 4096]))

            @blk.sync
            def _(sy, B=B, t0=t0):
                bar(sy, B)
                for t in range(8):
                    if t >= 2:
                        sy.wait_ge(s_dv.h, B[id(s_dv)] + t - 1)
                    sem = s_ld[t % 2]
                    sem.inc(sy.dma_start(out=ab[:, C_DTM + (t % 2) * 4096:C_DTM + (t % 2) * 4096 + 4096],
                                         in_=dfm_hb[:, t0 + t * 128:t0 + (t + 1) * 128], transpose=True), 16)
                    if t >= 2:
                        wt(sy, yst[t - 2])
                    sem.inc(sy.dma_start(out=af[:, (t % 2) * 4096:(t % 2) * 4096 + 4096],
                                         in_=x_e[t0 + t * 128:t0 + (t + 1) * 128, :]), 16)
                    old_.append([(sem, sem.n)])

            @blk.vector
            def _(v, B=B):
                bar(v, B)
                for t in range(8):
                    for ent in old_[t]:
                        wt(v, ent)
                    xb = af[:, (t % 2) * 4096:(t % 2) * 4096 + 4096]
                    s_dv.inc(v.tensor_add(xb, xb, ab[:, C_DTM + (t % 2) * 4096:C_DTM + (t % 2) * 4096 + 4096]))

    with nc.Block("fin") as blk:
        @blk.gpsimd
        def _(g):
            for s in [*s_stp, *s_ld, *s_cast, *s_st2, s_ac2]:
                if s.n > 0:
                    g.wait_ge(s.h, s.n)

    return nc


def _causal_masks():
    m = np.zeros((128, 2048), dtype=ml_dtypes.bfloat16)
    kk = np.arange(128)[:, None]
    qq = np.arange(512)[None, :]
    for mi in range(4):
        m[:, mi * 512:(mi + 1) * 512] = ((mi * 128 + kk) <= qq).astype(ml_dtypes.bfloat16)
    return m


def _host_prep(inputs):
    L = inputs
    cos2, sin2 = _rope_tables()
    f = np.float32
    common = dict(
        wq_img=_img(np.asarray(L["L0_attn_Wq"], f), D, D),
        wk_img=_img(np.asarray(L["L0_attn_Wk"], f), D, DQ),
        wv_img=_img(np.asarray(L["L0_attn_Wv"], f), D, DQ),
        wo_img=_img(np.asarray(L["L0_attn_Wo"], f), D, D),
        w1_img=_img_w13(np.asarray(L["L0_ffn_w1q"], f)),
        w3_img=_img_w13(np.asarray(L["L0_ffn_w3q"], f)),
        w2_img=_img_w2(np.asarray(L["L0_ffn_w2q"], f)),
        a1_img=_img_a(np.asarray(L["L0_ffn_a1"], f)),
        a3_img=_img_a(np.asarray(L["L0_ffn_a3"], f)),
        a2_img=_grid_a2(np.asarray(L["L0_ffn_a2"], f)),
        b1t=_grid_bT(np.asarray(L["L0_ffn_b1"], f)),
        b3t=_grid_bT(np.asarray(L["L0_ffn_b3"], f)),
        b2t=np.ascontiguousarray(np.asarray(L["L0_ffn_b2"], f).T),
        cos2=cos2, sin2=sin2,
        ones_col=np.ones((128, 1), dtype=ml_dtypes.bfloat16),
        ones_row=np.ones((1, 128), dtype=f),
        shift=np.full((128, 1), -SHIFT, dtype=f),
        cmask=_causal_masks(),
    )
    x = np.asarray(L["L0_attn_resid"], f)
    in_maps = []
    for b in range(8):
        m = dict(common)
        m["x"] = np.ascontiguousarray(x[b])
        in_maps.append(m)
    return in_maps


def kernel(**inputs):
    in_maps = _host_prep(inputs)
    nc = build(taps=False)
    res = run_bass_kernel_spmd(nc, in_maps, core_ids=list(range(8)))
    out = np.stack([r["y"] for r in res.results], axis=0)
    return out


# version 25
# speedup vs baseline: 1.0826x; 1.0826x over previous
import math
import numpy as np
import ml_dtypes

import concourse.bass as bass
import concourse.mybir as mybir
from concourse.bass_utils import run_bass_kernel_spmd

F32 = mybir.dt.float32
BF16 = mybir.dt.bfloat16
AF = mybir.ActivationFunctionType
ALU = mybir.AluOpType
AX = mybir.AxisListType

T = 2048
D = 4096
DQ = 1024
HPAD = 2816
NG = 88
SHIFT = 12.0
ISQ = 1.0 / math.sqrt(128.0)
EPS = 1e-5

BF_COLS = 69632
C_XNF = 0
C_K = 32768
C_WQ = 49152
C_QEV = 57344
C_M1 = 59392
C_M2 = 61440
C_SPILL = 63488
C_COS = 65536
C_V = 0
C_PRB = 16384
C_QS = 18432
C_CTXO = 19456
C_MSK = 20480
C_XF = 0
C_YF = 8192
C_W1 = 30720
C_W3 = 32768
C_GATE = 34816
C_YT = 35840
C_W2 = 36864
C_DEV = 42496
C_CTX = 0
C_WO = 32768
C_DEV2 = 40960
C_DBK = 43008
C_DTM = 45056
C_LNJ = 0
C_LNO = 4096
C_XFB = 49152

F32_COLS = 9216


def _rope_tables():
    idx = np.arange(0, 128, 2, dtype=np.float64) / 128.0
    freq = 1.0 / (2.0 * math.pi) * (500000.0 ** (-idx))
    factor, lo, hi = 32.0, 1.0, 4.0
    L0 = 8192.0
    freq_low, freq_high = lo / L0, hi / L0
    freq_scaled = np.where(freq < freq_low, freq / factor, freq)
    smooth = np.clip((L0 * freq - lo) / (hi - lo), 0.0, 1.0)
    freq_smooth = (1.0 - smooth) * (freq / factor) + smooth * freq
    is_mid = (freq >= freq_low) & (freq <= freq_high)
    freq = np.where(is_mid, freq_smooth, freq_scaled)
    phase = 2.0 * math.pi * np.arange(T, dtype=np.float64)[:, None] * freq[None, :]
    c = np.cos(phase).T.astype(np.float32)
    s = np.sin(phase).T.astype(np.float32)
    return (np.ascontiguousarray(np.concatenate([c, c], axis=0)),
            np.ascontiguousarray(np.concatenate([-s, s], axis=0)))


def _img(w, din, dout):
    return np.ascontiguousarray(
        w.reshape(din // 128, 128, dout // 128, 128).transpose(2, 1, 0, 3).reshape(dout // 128, 128, din))


def _img_w13(w):
    wg = w.reshape(4, 2752, DQ)
    wp = np.zeros((4, HPAD, DQ), dtype=w.dtype)
    wp[:, :2752, :] = wg
    wp = wp.reshape(NG, 128, DQ)
    return np.ascontiguousarray(wp.reshape(NG, 128, 8, 128).transpose(0, 3, 2, 1).reshape(NG, 128, DQ))


def _img_w2(w):
    wp = np.zeros((D, HPAD), dtype=w.dtype)
    wp[:, :2752] = w
    return np.ascontiguousarray(
        wp.reshape(32, 128, 22, 128).transpose(0, 3, 2, 1).reshape(32, 128, HPAD))


def _img_a(a):
    n = a.shape[1]
    return np.ascontiguousarray(a.reshape(2, n // 128, 128).transpose(2, 1, 0).reshape(128, (n // 128) * 2))


def _grid_a2(a2):
    ap = np.zeros((2, 4, HPAD), dtype=a2.dtype)
    ap[:, :, :2752] = np.asarray(a2).reshape(2, 4, 2752)
    return _img_a(np.ascontiguousarray(ap.reshape(2, NG * 128)))


def _grid_bT(b):
    bp = np.zeros((4, HPAD, 2), dtype=b.dtype)
    bp[:, :2752, :] = np.asarray(b).reshape(4, 2752, 2)
    return np.ascontiguousarray(bp.reshape(NG * 128, 2).T)




class _SkipBlock:
    def __enter__(self):
        return self

    def __exit__(self, *a):
        return False

    def __getattr__(self, name):
        return lambda f: None

class Sem:
    def __init__(self, h):
        self.h = h
        self.n = 0

    def inc(self, inst, k=1):
        inst.then_inc(self.h, k)
        self.n += k
        return self.n


def _attn_positions(npair):
    scpos = [0] * npair
    ctxpos = [0] * npair
    pos = 1
    scpos[0] = pos
    for p in range(npair):
        if p + 1 < npair:
            pos += 1
            scpos[p + 1] = pos
        pos += 1
        ctxpos[p] = pos
    rbpos = pos + 1
    return scpos, ctxpos, rbpos


PHASE_ORDER = ["setup", "ln0", "x0", "q0", "k0", "v0", "ln1", "x1", "q1", "k1", "v1",
               "attn", "ffn0", "wo0", "out0", "ffn1", "wo1", "out1", "fin"]


def build(taps=False, upto=None):
    lim = len(PHASE_ORDER) if upto is None else PHASE_ORDER.index(upto) + 1

    def on(nm):
        return PHASE_ORDER.index(nm) < lim
    nc = bass.Bass("TRN2", target_bir_lowering=False, debug=False)

    def din(name, shape, dt=F32):
        return nc.dram_tensor(name, shape, dt, kind="ExternalInput").ap()

    x_e = din("x", [T, D])
    wq_e = din("wq_img", [32, 128, D])
    wk_e = din("wk_img", [8, 128, D])
    wv_e = din("wv_img", [8, 128, D])
    wo_e = din("wo_img", [32, 128, D])
    w1_e = din("w1_img", [NG, 128, DQ])
    w3_e = din("w3_img", [NG, 128, DQ])
    w2_e = din("w2_img", [32, 128, HPAD])
    a1_e = din("a1_img", [128, 64])
    a3_e = din("a3_img", [128, 64])
    a2_e = din("a2_img", [128, NG * 2])
    b1_e = din("b1t", [2, NG * 128])
    b3_e = din("b3t", [2, NG * 128])
    b2_e = din("b2t", [2, D])
    cos_e = din("cos2", [128, T])
    sin_e = din("sin2", [128, T])
    onec_e = din("ones_col", [128, 1], BF16)
    oner_e = din("ones_row", [1, 128])
    shift_e = din("shift", [128, 1])
    cmask_e = din("cmask", [128, 2048], BF16)

    y_e = nc.dram_tensor("y", [T, D], F32, kind="ExternalOutput").ap()

    skind = "ExternalOutput" if taps else "Internal"
    xn_h = nc.dram_tensor("xn_hb", [T, D], BF16, kind=skind).ap()
    q_hb = nc.dram_tensor("q_hb", [32, 128, T], BF16, kind=skind).ap()
    vf_hb = nc.dram_tensor("vf_hb", [DQ, T], BF16, kind=skind).ap()
    ctx_hb = nc.dram_tensor("ctx_hb", [D, T], BF16, kind=skind).ap()
    dfm_hb = nc.dram_tensor("dfm_hb", [D, T], BF16, kind=skind).ap()
    xf_hb = nc.dram_tensor("xf_hb", [DQ, T], BF16, kind=skind).ap()
    if taps:
        k_hb = nc.dram_tensor("k_hb", [128, 16384], BF16, kind="ExternalOutput").ap()
        prb_hb = nc.dram_tensor("prb_hb", [128, 1024], BF16, kind="ExternalOutput").ap()

    ab = nc.alloc_sbuf_tensor("A_BF", [128, BF_COLS], BF16)
    af = nc.alloc_sbuf_tensor("A_F32", [128, F32_COLS], F32)
    barena = nc.alloc_sbuf_tensor("A_B", [128, NG * 128], BF16)
    laar = nc.alloc_sbuf_tensor("A_LA", [128, T], BF16)
    a1i = nc.alloc_sbuf_tensor("a1i", [128, 64], BF16)
    a3i = nc.alloc_sbuf_tensor("a3i", [128, 64], BF16)
    a2i = nc.alloc_sbuf_tensor("a2i", [128, NG * 2], BF16)
    onec = nc.alloc_sbuf_tensor("onec", [128, 1], BF16)
    oner = nc.alloc_sbuf_tensor("oner", [1, 128], F32)
    shif = nc.alloc_sbuf_tensor("shif", [128, 1], F32)
    s_sb = nc.alloc_sbuf_tensor("s_sb", [1, 1024], F32)
    r_sb = nc.alloc_sbuf_tensor("r_sb", [1, 1024], F32)
    st1 = nc.alloc_sbuf_tensor("st1", [128, 1], F32)
    st2 = nc.alloc_sbuf_tensor("st2", [128, 1], F32)
    stm = nc.alloc_sbuf_tensor("stm", [128, 1], F32)
    stv = nc.alloc_sbuf_tensor("stv", [128, 1], F32)
    str_ = nc.alloc_sbuf_tensor("str", [128, 1], F32)

    psa = nc.alloc_psum_tensor("PSA", [128, 2048], F32)
    psb = nc.alloc_psum_tensor("PSB", [128, 2048], F32)

    import contextlib
    es = contextlib.ExitStack()
    nc._es = es
    s_ld = [Sem(es.enter_context(nc.semaphore(f"s_ld{i}"))) for i in range(2)]
    s_cast = [Sem(es.enter_context(nc.semaphore(f"s_cast{i}"))) for i in range(2)]
    s_stp = [Sem(es.enter_context(nc.semaphore(f"s_stp{i}"))) for i in range(2)]
    s_dv2 = Sem(es.enter_context(nc.semaphore("s_dv2")))
    s_ac2 = Sem(es.enter_context(nc.semaphore("s_ac2")))
    s_st2 = [Sem(es.enter_context(nc.semaphore(f"s_st2{i}"))) for i in range(2)]
    s_pe = Sem(es.enter_context(nc.semaphore("s_pe")))
    s_dv = Sem(es.enter_context(nc.semaphore("s_dv")))
    s_ac = Sem(es.enter_context(nc.semaphore("s_ac")))
    ALL = [*s_ld, *s_cast, *s_stp, *s_st2, s_dv2, s_ac2, s_pe, s_dv, s_ac]

    def rec(lst, sems, inst):
        sem = sems[len(lst) % 2]
        sem.inc(inst, 16)
        lst.append((sem, sem.n))
        return lst[-1]

    def wt(eng, ent):
        eng.wait_ge(ent[0].h, ent[1])

    def wt_all(eng, lst):
        last = {}
        for sem, val in lst:
            last[id(sem)] = (sem, val)
        for sem, val in last.values():
            eng.wait_ge(sem.h, val)

    def snap():
        return {id(s): s.n for s in ALL}

    def bar(eng, B):
        for s in ALL:
            if B[id(s)] > 0:
                eng.wait_ge(s.h, B[id(s)])

    with nc.Block("setup") as blk:
        @blk.gpsimd
        def _(g):
            L = []
            for sname, dst in [(cos_e, ab[:, C_COS:C_COS + T]),
                               (sin_e, ab[:, C_COS + T:C_COS + 2 * T]),
                               (a1_e, a1i[:]), (a3_e, a3i[:]), (a2_e, a2i[:]),
                               (b1_e, barena[0:2, :]), (b3_e, barena[32:34, :]),
                               (b2_e, barena[64:66, 0:D])]:
                rec(L, s_cast, g.dma_start(out=dst, in_=sname))

        @blk.sync
        def _(sy):
            L = []
            rec(L, s_ld, sy.dma_start(out=onec[:], in_=onec_e))
            rec(L, s_ld, sy.dma_start(out=oner[:], in_=oner_e))
            rec(L, s_ld, sy.dma_start(out=shif[:], in_=shift_e))

    for h in range(2):
        t0 = h * 1024

        B = snap()
        xld, xst = [], []
        with (nc.Block(f"ln{h}") if (on(f"ln{h}") and h == 0) else _SkipBlock()) as blk:
            @blk.sync
            def _(sy, B=B, t0=t0):
                bar(sy, B)
                for t in range(8):
                    if t >= 2:
                        sy.wait_ge(s_dv.h, B[id(s_dv)] + (t - 1) * 8)
                    rec(xld, s_ld, sy.dma_start(out=af[:, (t % 2) * 4096:(t % 2) * 4096 + 4096],
                                                in_=x_e[t0 + t * 128: t0 + (t + 1) * 128, :]))

            @blk.gpsimd
            def _(g, B=B, t0=t0):
                bar(g, B)
                for t in range(8):
                    g.wait_ge(s_dv.h, B[id(s_dv)] + (t + 1) * 8)
                    rec(xst, s_stp, g.dma_start(out=xn_h[t0 + t * 128: t0 + (t + 1) * 128, :],
                                                in_=ab[:, C_LNO + (t % 2) * 4096:C_LNO + (t % 2) * 4096 + 4096]))

            @blk.vector
            def _(v, B=B):
                bar(v, B)
                for t in range(8):
                    xb = af[:, (t % 2) * 4096:(t % 2) * 4096 + 4096]
                    wt(v, xld[t])
                    if t >= 2:
                        wt(v, xst[t - 2])
                    s_dv.inc(v.reduce_sum(st1[:], xb, axis=AX.X))
                    s_dv.inc(v.scalar_tensor_tensor(ab[:, C_LNJ:C_LNJ + 4096], xb, 1.0, xb,
                                                    ALU.mult, ALU.mult, accum_out=st2[:]))
                    v.wait_ge(s_dv.h, s_dv.n)
                    s_dv.inc(v.tensor_scalar_mul(stm[:], st1[:], 1.0 / D))
                    s_dv.inc(v.tensor_scalar_mul(stv[:], st2[:], 1.0 / D))
                    s_dv.inc(v.tensor_mul(str_[:], stm[:], stm[:]))
                    s_dv.inc(v.tensor_sub(stv[:], stv[:], str_[:]))
                    s_dv.inc(v.tensor_scalar_add(stv[:], stv[:], EPS))
                    v.wait_ge(s_ac.h, B[id(s_ac)] + 2 * (t + 1))
                    s_dv.inc(v.tensor_scalar(ab[:, C_LNO + (t % 2) * 4096:C_LNO + (t % 2) * 4096 + 4096],
                                             xb, stm[:], str_[:], ALU.subtract, ALU.mult))

            @blk.scalar
            def _(sc, B=B):
                bar(sc, B)
                for t in range(8):
                    sc.wait_ge(s_dv.h, B[id(s_dv)] + t * 8 + 7)
                    s_ac.inc(sc.activation(st1[:], stv[:], AF.Ln))
                    sc.wait_ge(s_ac.h, s_ac.n)
                    s_ac.inc(sc.activation(str_[:], st1[:], AF.Exp, scale=-0.5))

        B = snap()
        tld, fst = [], []
        with (nc.Block(f"x{h}") if on(f"x{h}") else _SkipBlock()) as blk:
            @blk.sync
            def _(sy, B=B, t0=t0):
                bar(sy, B)
                for c in range(32):
                    rec(tld, s_ld, sy.dma_start(out=ab[:, C_XNF + c * 1024:C_XNF + (c + 1) * 1024],
                                                in_=xn_h[t0:t0 + 1024, c * 128:(c + 1) * 128], transpose=True))

            @blk.gpsimd
            def _(g, B=B, t0=t0):
                bar(g, B)
                for c8 in range(8):
                    g.wait_ge(s_dv.h, B[id(s_dv)] + (c8 + 1) * 3)
                    rec(fst, s_stp, g.dma_start(out=xf_hb[c8 * 128:(c8 + 1) * 128, t0:t0 + 1024],
                                                in_=ab[:, C_XFB + (c8 % 2) * 1024:C_XFB + (c8 % 2) * 1024 + 1024]))

            @blk.vector
            def _(v, B=B, t0=t0):
                bar(v, B)
                xn = lambda c: ab[:, C_XNF + c * 1024:C_XNF + (c + 1) * 1024]
                for c8 in range(8):
                    wt_all(v, tld)
                    t1 = af[:, 0:1024]
                    t2 = af[:, 1024:2048]
                    if c8 >= 2:
                        wt(v, fst[c8 - 2])
                    s_dv.inc(v.tensor_add(t1, xn(c8), xn(8 + c8)))
                    s_dv.inc(v.tensor_add(t2, xn(16 + c8), xn(24 + c8)))
                    s_dv.inc(v.tensor_add(ab[:, C_XFB + (c8 % 2) * 1024:C_XFB + (c8 % 2) * 1024 + 1024], t1, t2))
                v.wait_ge(s_pe.h, B[id(s_pe)] + 1)
                s_dv.inc(v.tensor_scalar_mul(laar[0:2, t0:t0 + 1024], psb[0:2, 0:1024], 0.5))
                v.wait_ge(s_pe.h, B[id(s_pe)] + 2)
                s_dv.inc(v.tensor_scalar_mul(laar[32:34, t0:t0 + 1024], psb[0:2, 1024:2048], 0.5))

            @blk.tensor
            def _(te, B=B):
                bar(te, B)
                wt_all(te, tld)
                for which, img in ((0, a1i), (1, a3i)):
                    ps = psb[0:2, 0:1024] if which == 0 else psb[0:2, 1024:2048]
                    mm = None
                    for c in range(32):
                        for n in range(2):
                            mm = te.matmul(ps[:, n * 512:(n + 1) * 512], img[:, c * 2:c * 2 + 2],
                                           ab[:, C_XNF + c * 1024 + n * 512:C_XNF + c * 1024 + n * 512 + 512],
                                           start=(c == 0), stop=(c == 31))
                    s_pe.inc(mm)

        for proj, no, img_e, rope in (("q", 32, wq_e, True), ("k", 8, wk_e, True), ("v", 8, wv_e, False)):
            B = snap()
            sst = []
            gln = (proj == "q" and h == 0)
            gxld, gxst = [], []
            LB = dict(dv2=s_dv2.n, ac2=s_ac2.n)

            def gln_stats(v, t):
                wt(v, gxld[t])
                if t >= 2:
                    wt(v, gxst[t - 2])
                xb = af[:, (t % 2) * 4096:(t % 2) * 4096 + 4096]
                s_dv2.inc(v.reduce_sum(st1[:], xb, axis=AX.X))
                s_dv2.inc(v.scalar_tensor_tensor(ab[:, C_K:C_K + 4096], xb, 1.0, xb,
                                                 ALU.mult, ALU.mult, accum_out=st2[:]))
                v.wait_ge(s_dv2.h, s_dv2.n)
                v.tensor_scalar_mul(stm[:], st1[:], 1.0 / D)
                v.tensor_mul(str_[:], stm[:], stm[:])
                v.tensor_scalar_mul(stv[:], st2[:], 1.0 / D)
                v.tensor_sub(stv[:], stv[:], str_[:])
                s_dv2.inc(v.tensor_scalar_add(stv[:], stv[:], EPS))

            def gln_norm(v, t):
                v.wait_ge(s_ac2.h, LB["ac2"] + 2 * (t + 1))
                if t >= 2:
                    wt(v, gxst[t - 2])
                xb = af[:, (t % 2) * 4096:(t % 2) * 4096 + 4096]
                nb = ab[:, C_K + 4096 + (t % 2) * 4096:C_K + 8192 + (t % 2) * 4096]
                s_dv2.inc(v.tensor_scalar(nb, xb, stm[:], str_[:], ALU.subtract, ALU.mult))

            def gln_act(sc, t):
                sc.wait_ge(s_dv2.h, LB["dv2"] + 4 * t + 3)
                s_ac2.inc(sc.activation(st1[:], stv[:], AF.Ln))
                sc.wait_ge(s_ac2.h, s_ac2.n)
                s_ac2.inc(sc.activation(str_[:], st1[:], AF.Exp, scale=-0.5))

            def gln_store(sy, t, B):
                sy.wait_ge(s_dv2.h, LB["dv2"] + 4 * (t + 1))
                sem2 = s_st2[t % 2]
                if t >= 2:
                    sy.wait_ge(sem2.h, B[id(sem2)] + 16 * (t // 2))
                sem2.inc(sy.dma_start(out=xn_h[1024 + t * 128:1024 + (t + 1) * 128, :],
                                      in_=ab[:, C_K + 4096 + (t % 2) * 4096:C_K + 8192 + (t % 2) * 4096]), 16)
                gxst.append((sem2, sem2.n))
            wld = {}
            with (nc.Block(f"{proj}{h}") if on(f"{proj}{h}") else _SkipBlock()) as blk:
                @blk.gpsimd
                def _(g, B=B, no=no, img_e=img_e):
                    bar(g, B)
                    for o in range(no):
                        if o >= 2:
                            g.wait_ge(s_pe.h, B[id(s_pe)] + o - 1)
                        sem = s_cast[o % 2]
                        base = C_WQ + (o % 2) * 4096
                        sem.inc(g.dma_start(out=ab[0:64, base:base + 4096], in_=img_e[o][0:64]), 16)
                        sem.inc(g.dma_start(out=ab[64:128, base:base + 4096], in_=img_e[o][64:128]), 16)
                        wld[o] = (sem, sem.n)

                @blk.sync
                def _(sy, B=B, no=no, proj=proj, t0=t0):
                    bar(sy, B)
                    if gln:
                        for t in range(8):
                            if t >= 2:
                                gln_store(sy, t - 2, B)
                            sem = s_ld[t % 2]
                            sem.inc(sy.dma_start(out=af[:, (t % 2) * 4096:(t % 2) * 4096 + 4096],
                                                 in_=x_e[1024 + t * 128:1024 + (t + 1) * 128, :]), 16)
                            gxld.append((sem, sem.n))
                        gln_store(sy, 6, B)
                        gln_store(sy, 7, B)
                    if proj == "v":
                        for o in range(no):
                            sb = ab[:, C_QEV + (o % 4) * 1024:C_QEV + (o % 4) * 1024 + 1024]
                            sy.wait_ge(s_ac.h, B[id(s_ac)] + o + 1)
                            rec(sst, s_stp, sy.dma_start(out=vf_hb[o * 128:(o + 1) * 128, t0:t0 + 1024], in_=sb))

                @blk.scalar
                def _(sc, B=B, no=no, rope=rope, proj=proj, t0=t0):
                    bar(sc, B)
                    for o in range(no):
                        if proj == "q" and o >= 1:
                            sc.wait_ge(s_dv.h, B[id(s_dv)] + o * 5)
                            rec(sst, s_stp, sc.dma_start(out=q_hb[o - 1][:, t0:t0 + 1024],
                                                         in_=ab[:, C_SPILL + ((o - 1) % 2) * 1024:C_SPILL + ((o - 1) % 2) * 1024 + 1024]))
                        if gln and o % 4 == 3:
                            gln_act(sc, o // 4)
                        sc.wait_ge(s_pe.h, B[id(s_pe)] + o + 1)
                        ps = psa[:, (o % 2) * 1024:(o % 2) * 1024 + 1024]
                        if rope:
                            if o >= 2:
                                sc.wait_ge(s_dv.h, B[id(s_dv)] + (o - 1) * 5)
                            s_ac.inc(sc.copy(ab[:, C_QEV + (o % 2) * 1024:C_QEV + (o % 2) * 1024 + 1024], ps))
                        else:
                            if o >= 4:
                                wt(sc, sst[o - 4])
                            s_ac.inc(sc.copy(ab[:, C_QEV + (o % 4) * 1024:C_QEV + (o % 4) * 1024 + 1024], ps))
                    if proj == "q":
                        sc.wait_ge(s_dv.h, B[id(s_dv)] + no * 5)
                        rec(sst, s_stp, sc.dma_start(out=q_hb[no - 1][:, t0:t0 + 1024],
                                                     in_=ab[:, C_SPILL + ((no - 1) % 2) * 1024:C_SPILL + ((no - 1) % 2) * 1024 + 1024]))

                @blk.tensor
                def _(te, B=B, no=no):
                    bar(te, B)
                    for o in range(no):
                        wt(te, wld[o])
                        if o >= 2:
                            te.wait_ge(s_ac.h, B[id(s_ac)] + o - 1)
                        ps = psa[:, (o % 2) * 1024:(o % 2) * 1024 + 1024]
                        wb = ab[:, C_WQ + (o % 2) * 4096:C_WQ + (o % 2) * 4096 + 4096]
                        mm = None
                        for d in range(32):
                            for n in range(2):
                                mm = te.matmul(ps[:, n * 512:(n + 1) * 512], wb[:, d * 128:(d + 1) * 128],
                                               ab[:, C_XNF + d * 1024 + n * 512:C_XNF + d * 1024 + n * 512 + 512],
                                               start=(d == 0), stop=(d == 31))
                        s_pe.inc(mm)

                if rope:
                    @blk.vector
                    def _(v, B=B, no=no, proj=proj, t0=t0):
                        bar(v, B)
                        cosb = ab[:, C_COS + t0:C_COS + t0 + 1024]
                        sinb = ab[:, C_COS + T + t0:C_COS + T + t0 + 1024]
                        for o in range(no):
                            if gln and o % 4 == 1:
                                gln_stats(v, o // 4)
                            if gln and o % 4 == 3:
                                gln_norm(v, o // 4)
                            v.wait_ge(s_ac.h, B[id(s_ac)] + o + 1)
                            qe = ab[:, C_QEV + (o % 2) * 1024:C_QEV + (o % 2) * 1024 + 1024]
                            m1 = ab[:, C_M1:C_M1 + 1024]
                            qsw = ab[:, C_M2:C_M2 + 1024]
                            if proj == "q":
                                dst = ab[:, C_SPILL + (o % 2) * 1024:C_SPILL + (o % 2) * 1024 + 1024]
                                if o >= 2:
                                    wt(v, sst[o - 2])
                            else:
                                dst = ab[:, C_K + o * 2048 + t0:C_K + o * 2048 + t0 + 1024]
                            s_dv.inc(v.tensor_copy(qsw[0:64, :], qe[64:128, :]))
                            s_dv.inc(v.tensor_copy(qsw[64:128, :], qe[0:64, :]))
                            s_dv.inc(v.tensor_mul(m1, qe, cosb))
                            s_dv.inc(v.tensor_mul(qe, qsw, sinb))
                            s_dv.inc(v.tensor_add(dst, m1, qe))

    B = snap()
    NKT = [4, 8, 12, 16]

    def np_of(it):
        return NKT[it % 4] // 2

    def pe_events():
        for j in range(64):
            A, Bv = 2 * j, 2 * j + 1
            np_A, np_B = np_of(A), np_of(Bv)
            if j == 0:
                yield ("sc", A, 0)
                yield ("sc", Bv, 0)
            for p in range(np_B):
                for it in (A, Bv):
                    if p < np_of(it):
                        if p + 1 < np_of(it):
                            yield ("sc", it, p + 1)
                        yield ("sctx", it, p)
                if p == np_A - 1 and A + 2 < 128:
                    yield ("sc", A + 2, 0)
                if p == np_A:
                    yield ("rb", A, 0)
            if Bv + 2 < 128:
                yield ("sc", Bv + 2, 0)
            yield ("rb", Bv, 0)

    def ac_events():
        for j in range(64):
            A, Bv = 2 * j, 2 * j + 1
            np_A, np_B = np_of(A), np_of(Bv)
            if j == 0:
                yield ("exp", A, 0)
                yield ("exp", Bv, 0)
            for p in range(1, np_B):
                for it in (A, Bv):
                    if p < np_of(it):
                        yield ("exp", it, p)
                if p == np_A - 1:
                    yield ("rcp", A, 0)
                if p == np_A and A + 2 < 128:
                    yield ("exp", A + 2, 0)
            yield ("rcp", Bv, 0)
            if Bv + 2 < 128:
                yield ("exp", Bv + 2, 0)

    def dv_events():
        for j in range(64):
            A, Bv = 2 * j, 2 * j + 1
            yield ("m", A, 0)
            yield ("m", A, 1)
            yield ("m", Bv, 0)
            yield ("m", Bv, 1)
            yield ("rbc", A, 0)
            yield ("tt", A, 0)
            yield ("rbc", Bv, 0)
            yield ("tt", Bv, 0)

    pe_pos, ac_pos, dv_pos = {}, {}, {}
    n = B[id(s_pe)]
    for ev in pe_events():
        n += 1
        pe_pos[ev] = n
    n = B[id(s_ac)]
    for ev in ac_events():
        n += 2 if ev[0] == "rcp" else 1
        ac_pos[ev] = n
    n = B[id(s_dv)]
    for ev in dv_events():
        n += 1
        dv_pos[ev] = n
    vld, qld, cst = [], [], []

    def scbuf(it):
        return psa[:, (it % 2) * 1024:(it % 2) * 1024 + 1024]

    def prbuf(it):
        return ab[:, C_PRB + (it % 2) * 1024:C_PRB + (it % 2) * 1024 + 1024]

    def ctxbuf(it):
        return psb[:, (it % 2) * 512:(it % 2) * 512 + 512]

    def srbbuf(it):
        return psb[:, 1024 + (it % 2) * 512:1024 + (it % 2) * 512 + 512]

    def qsbuf(it):
        return ab[:, 22528 + (it % 4) * 512:22528 + (it % 4) * 512 + 512]

    def rbsb(it):
        return af[:, 6144 + (it % 2) * 512:6144 + (it % 2) * 512 + 512]

    def ctxo(it):
        return ab[:, C_CTXO + (it % 2) * 512:C_CTXO + (it % 2) * 512 + 512]

    with (nc.Block("attn") if on("attn") else _SkipBlock()) as blk:
        @blk.sync
        def _(sy, B=B):
            bar(sy, B)
            rec(vld, s_ld, sy.dma_start(out=ab[:, C_MSK:C_MSK + 2048], in_=cmask_e))
            for kt in range(16):
                rec(vld, s_ld, sy.dma_start(out=ab[:, C_V + kt * 1024:C_V + (kt + 1) * 1024],
                                            in_=vf_hb[:, kt * 128:(kt + 1) * 128], transpose=True))
            wt_all(sy, vld)
            QSEM = [s_ld[0], s_ld[1], s_cast[0], s_cast[1]]
            for it in range(128):
                hq, qb = it // 4, it % 4
                if it >= 4:
                    sy.wait_ge(s_pe.h, pe_pos[("sc", it - 4, np_of(it - 4) - 1)])
                sem = QSEM[it % 4]
                sem.inc(sy.dma_start(out=qsbuf(it), in_=q_hb[hq][:, qb * 512:(qb + 1) * 512]), 16)
                qld.append((sem, sem.n))

        @blk.tensor
        def _(te, B=B):
            bar(te, B)
            wt_all(te, vld)
            qwaited = set()
            for ev in pe_events():
                tag, it, p = ev
                g = (it // 4) // 4
                nkt = NKT[it % 4]
                if tag == "sc":
                    if it not in qwaited:
                        wt(te, qld[it])
                        qwaited.add(it)
                    if p == 0 and it >= 2:
                        te.wait_ge(s_ac.h, ac_pos[("exp", it - 2, np_of(it - 2) - 1)])
                    if p >= 1:
                        te.wait_ge(s_ac.h, ac_pos[("exp", it, p - 1)])
                    scps = scbuf(it)
                    mm = None
                    for jj in range(2):
                        kt = 2 * p + jj
                        klhs = ab[:, C_K + g * 2048 + kt * 128:C_K + g * 2048 + kt * 128 + 128]
                        mm = te.matmul(scps[:, jj * 512:jj * 512 + 512], klhs, qsbuf(it), start=True, stop=True)
                    s_pe.inc(mm)
                elif tag == "rb":
                    te.wait_ge(s_ac.h, ac_pos[("rcp", it, 0)])
                    s_pe.inc(te.matmul(srbbuf(it), oner[:], r_sb[0:1, (it % 2) * 512:(it % 2) * 512 + 512],
                                       start=True, stop=True))
                elif tag == "sctx":
                    te.wait_ge(s_ac.h, ac_pos[("exp", it, p)])
                    npair = nkt // 2
                    if p >= npair - 2:
                        te.wait_ge(s_dv.h, dv_pos[("m", it, p - (npair - 2))])
                    if p == 0 and it >= 2:
                        te.wait_ge(s_dv.h, dv_pos[("tt", it - 2, 0)])
                    prb = prbuf(it)
                    mm = None
                    for jj in range(2):
                        kt = 2 * p + jj
                        vl = ab[:, C_V + kt * 1024 + g * 128:C_V + kt * 1024 + g * 128 + 128]
                        te.matmul(srbbuf(it)[0:1, :], onec[:], prb[:, jj * 512:jj * 512 + 512],
                                  start=(kt == 0), stop=(kt == nkt - 1))
                        mm = te.matmul(ctxbuf(it), vl, prb[:, jj * 512:jj * 512 + 512],
                                       start=(kt == 0), stop=(kt == nkt - 1))
                    s_pe.inc(mm)

        @blk.scalar
        def _(sc, B=B):
            bar(sc, B)
            for ev in ac_events():
                tag, it, p = ev
                if tag == "rcp":
                    sc.wait_ge(s_pe.h, pe_pos[("sctx", it, np_of(it) - 1)])
                    if it >= 2:
                        sc.wait_ge(s_pe.h, pe_pos[("rb", it - 2, 0)])
                    sslot = s_sb[0:1, (it % 2) * 512:(it % 2) * 512 + 512]
                    s_ac.inc(sc.activation(sslot, srbbuf(it)[0:1, :], AF.Ln))
                    sc.wait_ge(s_ac.h, s_ac.n)
                    s_ac.inc(sc.activation(r_sb[0:1, (it % 2) * 512:(it % 2) * 512 + 512],
                                           sslot, AF.Exp, scale=-1.0))
                    continue
                if p >= 1:
                    sc.wait_ge(s_pe.h, pe_pos[("sctx", it, p - 1)])
                else:
                    sc.wait_ge(s_pe.h, pe_pos[("sc", it, 0)])
                    if it >= 2:
                        sc.wait_ge(s_pe.h, pe_pos[("sctx", it - 2, np_of(it - 2) - 1)])
                s_ac.inc(sc.activation(prbuf(it), scbuf(it), AF.Exp, bias=shif[:], scale=ISQ))

        @blk.gpsimd
        def _(g, B=B):
            bar(g, B)
            if taps:
                g.wait_ge(s_dv.h, B[id(s_dv)])
                kst = []
                rec(kst, s_stp, g.dma_start(out=k_hb, in_=ab[:, C_K:C_K + 16384]))
                g.wait_ge(s_dv.h, dv_pos[("m", 0, 0)])
                rec(kst, s_stp, g.dma_start(out=prb_hb, in_=ab[:, C_PRB:C_PRB + 1024]))
            for it in range(128):
                hq, qb = it // 4, it % 4
                g.wait_ge(s_dv.h, dv_pos[("tt", it, 0)])
                rec(cst, s_stp, g.dma_start(out=ctx_hb[hq * 128:(hq + 1) * 128, qb * 512:(qb + 1) * 512],
                                            in_=ctxo(it)), )

        @blk.vector
        def _(v, B=B):
            bar(v, B)
            for ev in dv_events():
                tag, it, di = ev
                npair = np_of(it)
                if tag == "rbc":
                    v.wait_ge(s_pe.h, pe_pos[("rb", it, 0)])
                    s_dv.inc(v.tensor_copy(rbsb(it), srbbuf(it)[:]))
                elif tag == "tt":
                    if it >= 2:
                        wt(v, cst[it - 2])
                    s_dv.inc(v.tensor_mul(ctxo(it), ctxbuf(it)[:], rbsb(it)))
                elif tag == "m":
                    p = npair - 2 + di
                    v.wait_ge(s_ac.h, ac_pos[("exp", it, p)])
                    prb = prbuf(it)
                    mm = None
                    for jj in range(2):
                        mi = 2 * di + jj
                        mm = v.tensor_mul(prb[:, jj * 512:jj * 512 + 512],
                                          prb[:, jj * 512:jj * 512 + 512],
                                          ab[:, C_MSK + mi * 512:C_MSK + mi * 512 + 512])
                    s_dv.inc(mm)

    B_pre_out = [None]
    for h in range(2):
        t0 = h * 1024

        B = snap()
        NU = 176
        BPO = B_pre_out[0]
        xfld, w13ld, w2ld, dst_ = [], [], {}, []
        ghost = (h == 1)
        go_ld, go_st = [], []
        GB = dict(dv2=s_dv2.n)

        def ffn_entry(eng, B=B, BPO=BPO, h=h):
            import os as _os
            if h == 1 and BPO is not None and _os.environ.get("NO_OVL", "0") != "1":
                for s, key in ((s_pe, id(s_pe)), (s_dv, id(s_dv)), (s_ac, id(s_ac))):
                    if BPO[key] > 0:
                        eng.wait_ge(s.h, BPO[key])
                for s in (*s_stp, *s_cast):
                    if BPO[id(s)] > 0:
                        eng.wait_ge(s.h, BPO[id(s)])
            else:
                bar(eng, B)
        with (nc.Block(f"ffn{h}") if on(f"ffn{h}") else _SkipBlock()) as blk:
            @blk.gpsimd
            def _(g, B=B, t0=t0):
                ffn_entry(g)
                for idx in range(NG):
                    if idx >= 2:
                        g.wait_ge(s_pe.h, B[id(s_pe)] + 4 * idx - 6)
                    sem = s_cast[idx % 2]
                    gi = (idx % 4) * 22 + idx // 4
                    b1 = C_W1 + (idx % 2) * 1024
                    b3 = C_W3 + (idx % 2) * 1024
                    sem.inc(g.dma_start(out=ab[0:64, b1:b1 + 1024], in_=w1_e[gi][0:64]), 16)
                    sem.inc(g.dma_start(out=ab[64:128, b1:b1 + 1024], in_=w1_e[gi][64:128]), 16)
                    sem.inc(g.dma_start(out=ab[0:64, b3:b3 + 1024], in_=w3_e[gi][0:64]), 16)
                    sem.inc(g.dma_start(out=ab[64:128, b3:b3 + 1024], in_=w3_e[gi][64:128]), 16)
                    w13ld.append([(sem, sem.n)])
                for o in range(32):
                    if o >= 2:
                        g.wait_ge(s_pe.h, B[id(s_pe)] + 2 * NU + o - 1)
                    sem = s_cast[o % 2]
                    base = C_W2 + (o % 2) * HPAD
                    sem.inc(g.dma_start(out=ab[0:64, base:base + HPAD], in_=w2_e[o][0:64]), 16)
                    sem.inc(g.dma_start(out=ab[64:128, base:base + HPAD], in_=w2_e[o][64:128]), 16)
                    w2ld[o] = (sem, sem.n)

            @blk.scalar
            def _(sc, B=B):
                ffn_entry(sc)
                def ystore(t):
                    sc.wait_ge(s_dv2.h, GB["dv2"] + 4 * (t + 1))
                    sem2 = s_st2[t % 2]
                    sem2.inc(sc.dma_start(out=y_e[t * 128:(t + 1) * 128, :],
                                          in_=af[:, (t % 2) * 4096:(t % 2) * 4096 + 4096]), 16)
                    go_st.append((sem2, sem2.n))
                pe_a = lambda u: (2 * u) if u >= 1 else 1
                for u in range(NU):
                    if ghost and u % 22 == 0 and u >= 22:
                        ystore(u // 22 - 1)
                    upar = u % 2
                    if u >= 2:
                        sc.wait_ge(s_dv.h, B[id(s_dv)] + u - 1)
                    sc.wait_ge(s_pe.h, B[id(s_pe)] + pe_a(u))
                    s_ac.inc(sc.activation(ab[:, C_GATE + upar * 512:C_GATE + upar * 512 + 512],
                                           psa[:, upar * 512:upar * 512 + 512], AF.Silu))
                if ghost:
                    ystore(7)
                for o in range(32):
                    sc.wait_ge(s_pe.h, B[id(s_pe)] + 2 * NU + o + 1)
                    if o >= 2:
                        wt(sc, dst_[o - 2])
                    s_ac.inc(sc.copy(ab[:, C_DEV + (o % 2) * 1024:C_DEV + (o % 2) * 1024 + 1024],
                                     psa[:, (o % 2) * 1024:(o % 2) * 1024 + 1024]))
                    rec(dst_, s_stp, sc.dma_start(out=dfm_hb[o * 128:(o + 1) * 128, t0:t0 + 1024],
                                                  in_=ab[:, C_DEV + (o % 2) * 1024:C_DEV + (o % 2) * 1024 + 1024]))

            @blk.sync
            def _(sy, B=B, t0=t0):
                ffn_entry(sy)
                for c8 in range(8):
                    rec(xfld, s_ld, sy.dma_start(out=ab[:, C_XF + c8 * 1024:C_XF + (c8 + 1) * 1024],
                                                 in_=xf_hb[c8 * 128:(c8 + 1) * 128, t0:t0 + 1024]))
                if ghost:
                    wt_all(sy, xfld)
                    for t in range(8):
                        if t >= 2:
                            sy.wait_ge(s_dv2.h, GB["dv2"] + 4 * (t - 1))
                        sem = s_ld[t % 2]
                        sem.inc(sy.dma_start(out=ab[:, C_DTM + (t % 2) * 4096:C_DTM + (t % 2) * 4096 + 4096],
                                             in_=dfm_hb[:, t * 128:(t + 1) * 128], transpose=True), 16)
                        if t >= 2:
                            wt(sy, go_st[t - 2])
                        sem.inc(sy.dma_start(out=af[:, (t % 2) * 4096:(t % 2) * 4096 + 4096],
                                             in_=x_e[t * 128:(t + 1) * 128, :]), 16)
                        go_ld.append((sem, sem.n))

            @blk.tensor
            def _(te, B=B, t0=t0):
                ffn_entry(te)
                wt_all(te, xfld)
                for u in range(NU):
                    idx, tb = u // 2, u % 2
                    cp, r = idx // 4, idx % 4
                    upar = u % 2
                    b1ps = psa[:, upar * 512:upar * 512 + 512]
                    b3ps = psa[:, 1024 + upar * 512:1024 + upar * 512 + 512]
                    for ent in w13ld[idx]:
                        wt(te, ent)
                    if u >= 2:
                        te.wait_ge(s_ac.h, B[id(s_ac)] + u - 1)
                        te.wait_ge(s_dv.h, B[id(s_dv)] + u - 1)
                    w1b = ab[:, C_W1 + (idx % 2) * 1024:C_W1 + (idx % 2) * 1024 + 1024]
                    w3b = ab[:, C_W3 + (idx % 2) * 1024:C_W3 + (idx % 2) * 1024 + 1024]
                    g_ = r * 22 + cp
                    mm = None
                    for wb, bps, brow in ((w1b, b1ps, 0), (w3b, b3ps, 32)):
                        for d in range(8):
                            mm = te.matmul(bps, wb[:, d * 128:(d + 1) * 128],
                                           ab[:, C_XF + d * 1024 + tb * 512:C_XF + d * 1024 + tb * 512 + 512],
                                           start=(d == 0), stop=False)
                        mm = te.matmul(bps, barena[brow:brow + 2, g_ * 128:(g_ + 1) * 128],
                                       laar[brow:brow + 2, t0 + tb * 512:t0 + tb * 512 + 512],
                                       start=False, stop=True)
                    s_pe.inc(mm)
                    if u >= 1:
                        um = u - 1
                        idxm, tbm = um // 2, um % 2
                        gm = (idxm % 4) * 22 + idxm // 4
                        te.wait_ge(s_dv.h, B[id(s_dv)] + um + 1)
                        s_pe.inc(te.matmul(psb[0:2, tbm * 512:tbm * 512 + 512],
                                           a2i[:, gm * 2:gm * 2 + 2],
                                           ab[:, C_YT + (um % 2) * 512:C_YT + (um % 2) * 512 + 512],
                                           start=(idxm == 0), stop=(idxm == NG - 1)))
                um = NU - 1
                idxm, tbm = um // 2, um % 2
                gm = (idxm % 4) * 22 + idxm // 4
                te.wait_ge(s_dv.h, B[id(s_dv)] + um + 1)
                s_pe.inc(te.matmul(psb[0:2, tbm * 512:tbm * 512 + 512],
                                   a2i[:, gm * 2:gm * 2 + 2],
                                   ab[:, C_YT + (um % 2) * 512:C_YT + (um % 2) * 512 + 512],
                                   start=(idxm == 0), stop=(idxm == NG - 1)))
                for o in range(32):
                    wt(te, w2ld[o])
                    if o >= 2:
                        te.wait_ge(s_ac.h, B[id(s_ac)] + NU + o - 1)
                    if o == 0:
                        te.wait_ge(s_dv.h, B[id(s_dv)] + NU + 1)
                    ps = psa[:, (o % 2) * 1024:(o % 2) * 1024 + 1024]
                    w2b = ab[:, C_W2 + (o % 2) * HPAD:C_W2 + (o % 2) * HPAD + HPAD]
                    mm = None
                    for n in range(2):
                        for cp in range(22):
                            mm = te.matmul(ps[:, n * 512:n * 512 + 512], w2b[:, cp * 128:(cp + 1) * 128],
                                           ab[:, C_YF + cp * 1024 + n * 512:C_YF + cp * 1024 + n * 512 + 512],
                                           start=(cp == 0), stop=False)
                        mm = te.matmul(ps[:, n * 512:n * 512 + 512], barena[64:66, o * 128:(o + 1) * 128],
                                       laar[64:66, t0 + n * 512:t0 + n * 512 + 512],
                                       start=False, stop=True)
                    s_pe.inc(mm)

            @blk.vector
            def _(v, B=B, t0=t0):
                ffn_entry(v)
                for u in range(NU):
                    if ghost and (u % 22) in (6, 11, 16, 21) and u // 22 < 8:
                        t = u // 22
                        c = (6, 11, 16, 21).index(u % 22)
                        if c == 0:
                            wt(v, go_ld[t])
                        xb2 = af[:, (t % 2) * 4096 + c * 1024:(t % 2) * 4096 + c * 1024 + 1024]
                        s_dv2.inc(v.tensor_add(xb2, xb2,
                                               ab[:, C_DTM + (t % 2) * 4096 + c * 1024:C_DTM + (t % 2) * 4096 + c * 1024 + 1024]))
                    idx, tb = u // 2, u % 2
                    r = idx % 4
                    cp = idx // 4
                    upar = u % 2
                    v.wait_ge(s_ac.h, B[id(s_ac)] + u + 1)
                    gate = ab[:, C_GATE + upar * 512:C_GATE + upar * 512 + 512]
                    b3ps = psa[:, 1024 + upar * 512:1024 + upar * 512 + 512]
                    yt = ab[:, C_YT + upar * 512:C_YT + upar * 512 + 512]
                    yac = af[:, 8192 + tb * 512:8192 + tb * 512 + 512]
                    if u >= 2:
                        v.wait_ge(s_pe.h, B[id(s_pe)] + 2 * (u - 2) + 3)
                    v.tensor_mul(yt, gate, b3ps)
                    if r == 0:
                        s_dv.inc(v.tensor_mul(yac, gate, b3ps))
                    elif r < 3:
                        s_dv.inc(v.tensor_add(yac, yac, yt))
                    else:
                        s_dv.inc(v.tensor_add(ab[:, C_YF + cp * 1024 + tb * 512:C_YF + cp * 1024 + tb * 512 + 512],
                                              yac, yt))
                v.wait_ge(s_pe.h, B[id(s_pe)] + 2 * NU)
                s_dv.inc(v.tensor_scalar_mul(laar[64:66, t0:t0 + 1024], psb[0:2, 0:1024], 0.5))

        B = snap()
        cld, dld, wst = [], [], []
        wold = {}
        gwo = (h == 1)
        gp_ld = []
        GB2 = s_dv2.n
        AC2 = s_ac2.n
        with (nc.Block(f"wo{h}") if on(f"wo{h}") else _SkipBlock()) as blk:
            @blk.gpsimd
            def _(g, B=B, t0=t0):
                bar(g, B)
                def wstore(o):
                    g.wait_ge(s_dv.h, B[id(s_dv)] + o + 1)
                    rec(wst, s_stp, g.dma_start(out=dfm_hb[o * 128:(o + 1) * 128, t0:t0 + 1024],
                                                in_=ab[:, C_DEV2 + (o % 2) * 1024:C_DEV2 + (o % 2) * 1024 + 1024]))
                for o in range(32):
                    if o >= 2:
                        g.wait_ge(s_pe.h, B[id(s_pe)] + o - 1)
                    sem = s_cast[o % 2]
                    base = C_WO + (o % 2) * 4096
                    sem.inc(g.dma_start(out=ab[0:64, base:base + 4096], in_=wo_e[o][0:64]), 16)
                    sem.inc(g.dma_start(out=ab[64:128, base:base + 4096], in_=wo_e[o][64:128]), 16)
                    wold[o] = (sem, sem.n)
                    if o >= 1:
                        wstore(o - 1)
                wstore(31)

            @blk.scalar
            def _(sc, B=B, t0=t0):
                bar(sc, B)
                def ystore(t, hf):
                    sc.wait_ge(s_dv2.h, GB2 + 8 * hf + t + 1)
                    sc.wait_ge(s_ac2.h, AC2 + 128 * hf + 16 * t)
                    s_ac2.inc(sc.dma_start(out=y_e[t0 + t * 128:t0 + (t + 1) * 128, hf * 2048:hf * 2048 + 2048],
                                           in_=af[:, (t % 4) * 2048:(t % 4) * 2048 + 2048]), 16)
                if gwo:
                    for t in range(8):
                        ystore(t, 0)
                    for t in range(8):
                        ystore(t, 1)

            @blk.sync
            def _(sy, B=B, t0=t0):
                bar(sy, B)
                for c in range(32):
                    rec(cld, s_ld, sy.dma_start(out=ab[:, C_CTX + c * 1024:C_CTX + (c + 1) * 1024],
                                                in_=ctx_hb[c * 128:(c + 1) * 128, t0:t0 + 1024]))
                wt_all(sy, cld)
                def gpair(t, hf):
                    if hf == 0:
                        if t >= 4:
                            sy.wait_ge(s_ac2.h, AC2 + 16 * (t - 3))
                    else:
                        sy.wait_ge(s_ac2.h, AC2 + 128 + (16 * (t - 3) if t >= 4 else 0))
                    sem = s_st2[t % 2]
                    sem.inc(sy.dma_start(out=ab[:, C_DTM + (t % 4) * 2048:C_DTM + (t % 4) * 2048 + 2048],
                                         in_=dfm_hb[hf * 2048:hf * 2048 + 2048, t0 + t * 128:t0 + (t + 1) * 128],
                                         transpose=True), 16)
                    sem.inc(sy.dma_start(out=af[:, (t % 4) * 2048:(t % 4) * 2048 + 2048],
                                         in_=x_e[t0 + t * 128:t0 + (t + 1) * 128, hf * 2048:hf * 2048 + 2048]), 16)
                    gp_ld.append((sem, sem.n))
                for o in range(32):
                    if o >= 2:
                        sy.wait_ge(s_dv.h, B[id(s_dv)] + o - 1)
                    rec(dld, s_ld, sy.dma_start(out=ab[:, C_DBK + (o % 2) * 1024:C_DBK + (o % 2) * 1024 + 1024],
                                                in_=dfm_hb[o * 128:(o + 1) * 128, t0:t0 + 1024]))
                    if gwo and 19 <= o < 27:
                        if o == 19:
                            wt_all(sy, wst[:16])
                        gpair(o - 19, 0)
                if gwo:
                    wt_all(sy, wst)
                    for t in range(8):
                        gpair(t, 1)

            @blk.tensor
            def _(te, B=B):
                bar(te, B)
                wt_all(te, cld)
                for o in range(32):
                    wt(te, wold[o])
                    if o >= 2:
                        te.wait_ge(s_dv.h, B[id(s_dv)] + o - 1)
                    ps = psa[:, (o % 2) * 1024:(o % 2) * 1024 + 1024]
                    wb = ab[:, C_WO + (o % 2) * 4096:C_WO + (o % 2) * 4096 + 4096]
                    mm = None
                    for d in range(32):
                        for n in range(2):
                            mm = te.matmul(ps[:, n * 512:(n + 1) * 512], wb[:, d * 128:(d + 1) * 128],
                                           ab[:, C_CTX + d * 1024 + n * 512:C_CTX + d * 1024 + n * 512 + 512],
                                           start=(d == 0), stop=(d == 31))
                    s_pe.inc(mm)

            @blk.vector
            def _(v, B=B):
                bar(v, B)
                def gadd(t, hf):
                    wt(v, gp_ld[8 * hf + t])
                    xh = af[:, (t % 4) * 2048:(t % 4) * 2048 + 2048]
                    s_dv2.inc(v.tensor_add(xh, xh,
                                           ab[:, C_DTM + (t % 4) * 2048:C_DTM + (t % 4) * 2048 + 2048]))
                for o in range(32):
                    v.wait_ge(s_pe.h, B[id(s_pe)] + o + 1)
                    wt(v, dld[o])
                    if o >= 2:
                        wt(v, wst[o - 2])
                    s_dv.inc(v.tensor_add(ab[:, C_DEV2 + (o % 2) * 1024:C_DEV2 + (o % 2) * 1024 + 1024],
                                          psa[:, (o % 2) * 1024:(o % 2) * 1024 + 1024],
                                          ab[:, C_DBK + (o % 2) * 1024:C_DBK + (o % 2) * 1024 + 1024]))
                    if gwo and 21 <= o < 29:
                        gadd(o - 21, 0)
                if gwo:
                    for t in range(8):
                        gadd(t, 1)

        B = snap()
        B_pre_out[0] = dict(B)
        old_, yst, shalf = [], [], []
        OUT_SKIP = True
        with (nc.Block(f"out{h}") if (on(f"out{h}") and not OUT_SKIP) else _SkipBlock()) as blk:
            @blk.gpsimd
            def _(g, B=B, t0=t0):
                bar(g, B)
                for t in range(8):
                    g.wait_ge(s_dv.h, B[id(s_dv)] + t + 1)
                    rec(yst, s_stp, g.dma_start(out=y_e[t0 + t * 128:t0 + (t + 1) * 128, :],
                                                in_=af[:, (t % 2) * 4096:(t % 2) * 4096 + 4096]))

            @blk.sync
            def _(sy, B=B, t0=t0):
                bar(sy, B)
                for t in range(8):
                    if t >= 2:
                        sy.wait_ge(s_dv.h, B[id(s_dv)] + t - 1)
                    sem = s_ld[t % 2]
                    sem.inc(sy.dma_start(out=ab[:, C_DTM + (t % 2) * 4096:C_DTM + (t % 2) * 4096 + 4096],
                                         in_=dfm_hb[:, t0 + t * 128:t0 + (t + 1) * 128], transpose=True), 16)
                    if t >= 2:
                        wt(sy, yst[t - 2])
                    sem.inc(sy.dma_start(out=af[:, (t % 2) * 4096:(t % 2) * 4096 + 4096],
                                         in_=x_e[t0 + t * 128:t0 + (t + 1) * 128, :]), 16)
                    old_.append([(sem, sem.n)])

            @blk.vector
            def _(v, B=B):
                bar(v, B)
                for t in range(8):
                    for ent in old_[t]:
                        wt(v, ent)
                    xb = af[:, (t % 2) * 4096:(t % 2) * 4096 + 4096]
                    s_dv.inc(v.tensor_add(xb, xb, ab[:, C_DTM + (t % 2) * 4096:C_DTM + (t % 2) * 4096 + 4096]))

    with nc.Block("fin") as blk:
        @blk.gpsimd
        def _(g):
            for s in [*s_stp, *s_ld, *s_cast, *s_st2, s_ac2]:
                if s.n > 0:
                    g.wait_ge(s.h, s.n)

    return nc


def _causal_masks():
    m = np.zeros((128, 2048), dtype=ml_dtypes.bfloat16)
    kk = np.arange(128)[:, None]
    qq = np.arange(512)[None, :]
    for mi in range(4):
        m[:, mi * 512:(mi + 1) * 512] = ((mi * 128 + kk) <= qq).astype(ml_dtypes.bfloat16)
    return m


def _host_prep(inputs):
    L = inputs
    cos2, sin2 = _rope_tables()
    f = np.float32
    common = dict(
        wq_img=_img(np.asarray(L["L0_attn_Wq"], f), D, D),
        wk_img=_img(np.asarray(L["L0_attn_Wk"], f), D, DQ),
        wv_img=_img(np.asarray(L["L0_attn_Wv"], f), D, DQ),
        wo_img=_img(np.asarray(L["L0_attn_Wo"], f), D, D),
        w1_img=_img_w13(np.asarray(L["L0_ffn_w1q"], f)),
        w3_img=_img_w13(np.asarray(L["L0_ffn_w3q"], f)),
        w2_img=_img_w2(np.asarray(L["L0_ffn_w2q"], f)),
        a1_img=_img_a(np.asarray(L["L0_ffn_a1"], f)),
        a3_img=_img_a(np.asarray(L["L0_ffn_a3"], f)),
        a2_img=_grid_a2(np.asarray(L["L0_ffn_a2"], f)),
        b1t=_grid_bT(np.asarray(L["L0_ffn_b1"], f)),
        b3t=_grid_bT(np.asarray(L["L0_ffn_b3"], f)),
        b2t=np.ascontiguousarray(np.asarray(L["L0_ffn_b2"], f).T),
        cos2=cos2, sin2=sin2,
        ones_col=np.ones((128, 1), dtype=ml_dtypes.bfloat16),
        ones_row=np.ones((1, 128), dtype=f),
        shift=np.full((128, 1), -SHIFT, dtype=f),
        cmask=_causal_masks(),
    )
    x = np.asarray(L["L0_attn_resid"], f)
    in_maps = []
    for b in range(8):
        m = dict(common)
        m["x"] = np.ascontiguousarray(x[b])
        in_maps.append(m)
    return in_maps


def kernel(**inputs):
    in_maps = _host_prep(inputs)
    nc = build(taps=False)
    res = run_bass_kernel_spmd(nc, in_maps, core_ids=list(range(8)))
    out = np.stack([r["y"] for r in res.results], axis=0)
    return out
